# version 14
# baseline (speedup 1.0000x reference)
import contextlib
import numpy as np
import ml_dtypes
import concourse.bass as bass
import concourse.mybir as mybir
from concourse.bass_utils import run_bass_kernel_spmd

F32 = mybir.dt.float32
BF16 = mybir.dt.bfloat16
AF = mybir.ActivationFunctionType
ALU = mybir.AluOpType
AX = mybir.AxisListType

NCORES = 8
B_PER_CORE = 2
L = 2048
D = 2048
TPC = B_PER_CORE * L
NTT = TPC // 128

A_DI, A_H, A_N, A_K, A_DTR = 512, 32, 16, 4, 128
B_DI, B_DXB, B_N, B_DTR, B_K = 4096, 1024, 16, 128, 4
FF = 8192

ENGINES = ("sync", "tensor", "vector", "scalar", "gpsimd")


def _consts():
    bf = ml_dtypes.bfloat16
    eye = np.eye(128, dtype=np.float32).astype(bf)
    mx = np.zeros((32, 128), dtype=np.float32)
    for p_ in range(128):
        mx[16 * (p_ // 64) + p_ % 16, p_] = 1.0
    mb_ = np.zeros((32, 16 * 128), dtype=np.float32)
    for n in range(16):
        for p_ in range(128):
            mb_[n + 16 * (p_ // 64), n * 128 + p_] = 1.0
    mc = np.zeros((128, 16 * 128), dtype=np.float32)
    for n in range(16):
        for p_ in range(128):
            mc[16 * (p_ // 16) + n, n * 128 + p_] = 1.0
    md = np.zeros((32, 4 * 128), dtype=np.float32)
    for j in range(4):
        for p_ in range(128):
            md[8 * j + p_ // 16, j * 128 + p_] = 1.0
    ones = np.ones((128, 1), dtype=np.float32).astype(bf)
    c128 = np.full((128, 1), 128.0, dtype=np.float32)
    cmisc = np.zeros((128, 8), dtype=np.float32)
    cmisc[:, 0] = 1e-5
    cmisc[:, 1] = 1e-6
    cmisc[:, 2] = -0.5
    cmisc[:, 3] = -1e-6
    cmisc[:, 4] = 5e-7
    cmisc[:, 5] = 1.0
    onesr = np.ones((1, 128), dtype=np.float32)
    return dict(c_eye=eye, c_mx=mx.astype(bf), c_mb=mb_.astype(bf),
                c_mc=mc.astype(bf), c_md=md, c_ones=ones,
                c_128=c128, c_misc=cmisc, c_onesr=onesr)


class Sem:
    __slots__ = ("h", "n")

    def __init__(self, h):
        self.h = h
        self.n = 0

    def inc(self, k=1):
        self.n += k
        return self.n


class Prog:
    def __init__(self, nc, es):
        self.nc = nc
        self.es = es
        self.ops = []
        self._sems = {}
        self._lastwait = {}
        self._dma_rot = {}
        self._dirty = {e: False for e in ENGINES}

    def sem(self, name):
        if name not in self._sems:
            h = self.es.enter_context(self.nc.semaphore(name))
            self._sems[name] = Sem(h)
        return self._sems[name]

    def op(self, eng, fn, waits=(), inc=None, inc_k=1, kind="compute"):
        real = []
        for (s, v) in waits:
            if v <= 0:
                continue
            key = (eng, id(s))
            if self._lastwait.get(key, -1) >= v:
                continue
            self._lastwait[key] = v
            real.append((s, v))
        if inc is not None:
            inc.inc(inc_k)
            self._dirty[eng] = False
        elif kind == "compute":
            self._dirty[eng] = True
        self.ops.append((eng, fn, real, inc, inc_k, kind))
        return inc.n if inc is not None else None

    NDMASEM = 10

    def _dma_sems(self, q):
        key = "ld" if q == "sync" else "lg"
        return [self.sem(f"{key}{i}") for i in range(self.NDMASEM)]

    def dma(self, q, out_ap, in_ap, waits=(), **kw):
        sems = self._dma_sems(q)
        idx = self._dma_rot.get(q, 0)
        self._dma_rot[q] = (idx + 1) % self.NDMASEM
        sem = sems[idx]
        waits = tuple(waits) + ((sem, sem.n),)

        def fn(out_ap=out_ap, in_ap=in_ap, q=q, kw=kw):
            return getattr(self.nc, q).dma_start(out=out_ap, in_=in_ap, **kw)
        val = self.op(q, fn, waits=waits, inc=sem, inc_k=16, kind="dma")
        return (sem, val)

    def dma_marks(self, q=None):
        qs = ("sync", "gpsimd") if q is None else (q,)
        out = []
        for qq in qs:
            for s in self._dma_sems(qq):
                out.append((s, s.n))
        return tuple(out)

    ENG_MILESTONE = dict(tensor="sm", vector="sv", scalar="ss",
                         gpsimd="sp")

    def flush(self, eng):
        semname = self.ENG_MILESTONE.get(eng)
        if semname is None:
            return None
        sem = self.sem(semname)
        if not self._dirty[eng]:
            return (sem, sem.n)
        if eng == "tensor":
            def fn():
                return self.nc.tensor.nop()
        else:
            def fn(eng=eng):
                return getattr(self.nc, eng).drain()
        self.op(eng, fn, inc=sem, kind="raw")
        self._dirty[eng] = False
        return (sem, sem.n)

    def vmark(self, eng):
        return self.flush(eng)

    def fence(self, engines=ENGINES):
        marks = [self.flush(e) for e in ("tensor", "vector", "scalar",
                                         "gpsimd")]
        marks = [m for m in marks if m is not None] + list(self.dma_marks())
        for e in engines:
            self.op(e, lambda: None, waits=marks)

    def barrier(self):
        bar = self.sem("bar")
        for e in ("tensor", "vector", "scalar", "gpsimd"):
            self.flush(e)
        dma_marks = self.dma_marks()
        for e in ENGINES:
            def fn(e=e):
                return getattr(self.nc, e).sem_inc(bar.h, 1)
            self.op(e, fn, waits=dma_marks, kind="raw")
        bar.n += 5
        tgt = bar.n
        for e in ENGINES:
            self.op(e, lambda: None, waits=((bar, tgt),))

    def build(self, block):
        by_eng = {e: [] for e in ENGINES}
        for rec in self.ops:
            by_eng[rec[0]].append(rec)

        drain_engines = ("vector", "scalar", "gpsimd")

        def mk(e):
            def body(eng):
                for (_, fn, waits, inc, inc_k, kind) in by_eng[e]:
                    for (s, v) in waits:
                        eng.wait_ge(s.h, v)
                    ins = fn()
                    if kind == "compute" and ins is not None and                             e in drain_engines:
                        ins = eng.drain()
                    if inc is not None:
                        assert ins is not None
                        ins.then_inc(inc.h, inc_k)
            return body
        block.sync(mk("sync"))
        block.tensor(mk("tensor"))
        block.vector(mk("vector"))
        block.scalar(mk("scalar"))
        block.gpsimd(mk("gpsimd"))


def ln_stats(p, xt_ap, dlen, st, dump, eps, neghalf, waits=()):
    nc = p.nc
    sv, ss = p.sem("sv"), p.sem("ss")
    p.op("vector", lambda: nc.vector.reduce_sum(st[:, 0:1], xt_ap, axis=AX.X),
         waits=waits, inc=sv)
    p.op("scalar", lambda: nc.scalar.activation(
        out=dump[:, :dlen], in_=xt_ap, func=AF.Square, accum_out=st[:, 1:2]),
        waits=waits, inc=ss)
    vsq = ss.n
    inv = 1.0 / dlen
    p.op("vector", lambda: nc.vector.tensor_scalar(
        out=st[:, 0:1], in0=st[:, 0:1], scalar1=inv, scalar2=None,
        op0=ALU.mult))
    p.op("vector", lambda: nc.vector.tensor_scalar(
        out=st[:, 1:2], in0=st[:, 1:2], scalar1=inv, scalar2=None,
        op0=ALU.mult), waits=((ss, vsq),))
    p.op("vector", lambda: nc.vector.tensor_mul(
        st[:, 4:5], st[:, 0:1], st[:, 0:1]))
    p.op("vector", lambda: nc.vector.tensor_sub(
        st[:, 1:2], st[:, 1:2], st[:, 4:5]), inc=sv)
    vvar = sv.n
    p.op("scalar", lambda: nc.scalar.activation(
        out=st[:, 4:5], in_=st[:, 1:2], func=AF.Ln, bias=eps),
        waits=((sv, vvar),))
    p.op("scalar", lambda: nc.scalar.activation(
        out=st[:, 2:3], in_=st[:, 4:5], func=AF.Exp, scale=neghalf), inc=ss)
    vr = ss.n
    p.op("vector", lambda: nc.vector.tensor_scalar(
        out=st[:, 3:4], in0=st[:, 0:1], scalar1=st[:, 2:3], scalar2=-1.0,
        op0=ALU.mult, op1=ALU.mult), waits=((ss, vr),), inc=sv)
    return sv.n


def softplus(p, src_ap, bias_col, out_f32, tmp, waits=()):
    nc = p.nc
    sv, ss = p.sem("sv"), p.sem("ss")
    p.op("scalar", lambda: nc.scalar.activation(
        out=out_f32, in_=src_ap, func=AF.Identity, bias=bias_col),
        waits=waits)
    p.op("scalar", lambda: nc.scalar.activation(
        out=tmp, in_=out_f32, func=AF.Abs))
    p.op("scalar", lambda: nc.scalar.activation(
        out=tmp, in_=tmp, func=AF.Exp, scale=np.float32(-1.0)))
    p.op("scalar", lambda: nc.scalar.activation(
        out=tmp, in_=tmp, func=AF.Ln, bias=np.float32(1.0)), inc=ss)
    v1 = ss.n
    p.op("vector", lambda: nc.vector.tensor_scalar(
        out=out_f32, in0=out_f32, scalar1=0.0, scalar2=None, op0=ALU.max),
        waits=((ss, v1),))
    p.op("vector", lambda: nc.vector.tensor_add(out_f32, out_f32, tmp),
         inc=sv)
    return sv.n


class MK:

    def __init__(self, debug=()):
        self.debug = set(debug)
        nc = bass.Bass("TRN2", target_bir_lowering=False)
        self.nc = nc
        self.es = contextlib.ExitStack()
        self.p = Prog(nc, self.es)
        self._declare_io()
        self._declare_global_sbuf()

    def _declare_io(self):
        nc = self.nc
        inp = lambda n, s, d=F32: nc.dram_tensor(n, s, d,
                                                 kind="ExternalInput").ap()
        self.hs = inp("hs", [TPC, D])
        self.aWx = inp("L0_attn_a_W_in_x", [A_DI, D])
        self.aWz = inp("L0_attn_a_W_in_z", [A_DI, D])
        self.aCw = inp("L0_attn_a_conv_w", [A_DI, 1, A_K])
        self.aCb = inp("L0_attn_a_conv_b", [A_DI])
        self.aWp = inp("L0_attn_a_W_param", [A_DTR + 2 * A_H * A_N, A_DI])
        self.aWdt = inp("L0_attn_a_W_dt", [A_H, A_DTR])
        self.aBdt = inp("L0_attn_a_b_dt", [A_H])
        self.aAlog = inp("L0_attn_a_A_log", [A_H, A_N])
        self.aD = inp("L0_attn_a_D", [A_DI])
        self.aWo = inp("L0_attn_a_W_out", [D, A_DI])
        self.bWin = inp("L0_attn_b_in_proj_w", [2 * B_DXB + 2 * B_DI, D])
        self.bWdti = inp("L0_attn_b_dt_in_proj_w", [B_DTR, D])
        self.bWdtp = inp("L0_attn_b_dt_proj_w", [B_DI, B_DTR])
        self.bBdtp = inp("L0_attn_b_dt_proj_b", [B_DI])
        self.bCw = inp("L0_attn_b_conv_w", [B_DI, B_K])
        self.bCb = inp("L0_attn_b_conv_b", [B_DI])
        self.bAlog = inp("L0_attn_b_A_log", [B_DI, B_N])
        self.bD = inp("L0_attn_b_D", [B_DI])
        self.bWo = inp("L0_attn_b_out_proj_w", [D, B_DI])
        self.fWfc = inp("L0_ffn_w_fc", [FF, D])
        self.fWpj = inp("L0_ffn_w_proj", [D, FF])
        self.c_eye = inp("c_eye", [128, 128], BF16)
        self.c_mx = inp("c_mx", [32, 128], BF16)
        self.c_mb = inp("c_mb", [32, 16 * 128], BF16)
        self.c_mc = inp("c_mc", [128, 16 * 128], BF16)
        self.c_md = inp("c_md", [32, 4 * 128], F32)
        self.c_ones = inp("c_ones", [128, 1], BF16)
        self.c_128 = inp("c_128", [128, 1], F32)
        self.c_misc = inp("c_misc", [128, 8], F32)
        self.c_onesr = inp("c_onesr", [1, 128], F32)
        self.y_out = nc.dram_tensor("y", [TPC, D], F32,
                                    kind="ExternalOutput").ap()

        def scr(n, s, d):
            kind = "ExternalOutput" if n in self.debug else "Internal"
            return nc.dram_tensor(n, s, d, kind=kind).ap()
        self.x1b = scr("x1b", [TPC, D], F32)
        self.x2b = scr("x2b", [TPC, D], F32)
        self.x3b = scr("x3b", [TPC, D], F32)
        self.sp_inpT = scr("sp_inpT", [80, 128, 16, 128], BF16)
        self.sp_woT = scr("sp_woT", [2, 128, 32, 1024], BF16)
        self.sp_fcsT = scr("sp_fcsT", [4, 128, 16, D], BF16)
        self.sp_pjsT = scr("sp_pjsT", [2, 128, 64, 1024], BF16)
        self.sp_xhT = scr("sp_xhT", [NTT, 128, 16 * 128], BF16)
        self.sp_h = scr("sp_h", [TPC, FF], BF16)
        self.sp_hhT = scr("sp_hhT", [NTT, 128, 64 * 128], BF16)
        self.sp_sz = scr("sp_sz", [B_PER_CORE, B_DI, L], BF16)
        self.sp_x = scr("sp_x", [B_PER_CORE, B_DXB, L], BF16)
        self.sp_B = scr("sp_B", [B_PER_CORE, B_DXB, L], BF16)
        self.sp_C = scr("sp_C", [B_PER_CORE, B_DI, L], BF16)
        self.sp_y3 = scr("sp_y3", [B_PER_CORE * (L // 128) * B_DI * 128], BF16)
        self.dbg = scr("dbg", [128, 16384], F32)
        self.dbg2 = scr("dbg2", [128, 16384], BF16)

    def _declare_global_sbuf(self):
        nc, es = self.nc, self.es
        sb = lambda n, s, d: es.enter_context(nc.sbuf_tensor(n, s, d)).ap()
        self.eye = sb("eye", [128, 128], BF16)
        self.mx = sb("mx", [32, 128], BF16)
        self.mb = sb("mb", [32, 16 * 128], BF16)
        self.mc = sb("mc", [128, 16 * 128], BF16)
        self.md = sb("md", [32, 4 * 128], F32)
        self.ones = sb("ones", [128, 1], BF16)
        self.v128 = sb("v128", [128, 1], F32)
        self.cm = sb("cm", [128, 8], F32)
        self.onesr = sb("onesr", [1, 128], F32)
        self.qb = sb("qb", [128, 2], F32)
        self.gam = sb("gam", [128, 8], F32)
        self.dtinT = sb("dtinT", [128, 16 * 128], BF16)
        self.dtpjT = sb("dtpjT", [128, 32 * 128], BF16)
        self.b_Ab = sb("b_Ab", [128, 32 * 16], F32)
        self.b_misc = sb("b_misc", [128, 32 * 3], F32)
        self.b_convw = sb("b_convw", [128, 32 * 4], F32)

    def prep_transposed_sb(self, src, rows, cols, dst_sb, fstride, wload, wbf):
        p, nc = self.p, self.nc
        sv, ss, sm, ld = p.sem("sv"), p.sem("ss"), p.sem("sm"), p.sem("ld")
        for fi in range(rows // 128):
            lv = p.dma("sync", wload[:, :cols], src[fi * 128:(fi + 1) * 128, :],
                       waits=((ss, getattr(self, "_pt_cast", 0)),))
            p.op("scalar", lambda cols=cols: nc.scalar.activation(
                out=wbf[:, :cols], in_=wload[:, :cols], func=AF.Copy),
                waits=(lv, (sm, getattr(self, "_pt_tr", 0))), inc=ss)
            self._pt_cast = ss.n
            sc = ss.n
            nk = cols // 128
            dk = 0
            while dk < nk:
                g = min(8, nk - dk)
                for j in range(g):
                    def fn(dk=dk, j=j):
                        return nc.tensor.transpose(
                            self.ps_t[:, j * 128:(j + 1) * 128],
                            wbf[:, (dk + j) * 128:(dk + j + 1) * 128],
                            self.eye[:])
                    p.op("tensor", fn, waits=((ss, sc),),
                         inc=(p.sem("sm") if j == g - 1 else None))
                mmv = sm.n
                self._pt_tr = mmv
                for j in range(g):
                    def ev(dk=dk, j=j, fi=fi, fstride=fstride):
                        return nc.vector.tensor_copy(
                            dst_sb[:, (dk + j) * fstride + fi * 128:
                                   (dk + j) * fstride + (fi + 1) * 128],
                            self.ps_t[:, j * 128:(j + 1) * 128])
                    p.op("vector", ev, waits=((sm, mmv),),
                         inc=(sv if j == g - 1 else None))
                p.op("tensor", lambda: None, waits=((sv, sv.n),))
                dk += g

    def prologue_a(self):
        p, nc = self.p, self.nc
        sv, ss, sm, ld = p.sem("sv"), p.sem("ss"), p.sem("sm"), p.sem("ld")
        for dst, src in ((self.eye, self.c_eye), (self.mx, self.c_mx),
                         (self.mb, self.c_mb), (self.mc, self.c_mc),
                         (self.md, self.c_md), (self.ones, self.c_ones),
                         (self.v128, self.c_128), (self.cm, self.c_misc),
                         (self.onesr, self.c_onesr)):
            p.dma("sync", dst[:, :], src)
        p.op("scalar", lambda: nc.scalar.activation(
            out=self.qb[:, 0:1], in_=self.v128[:, :], func=AF.Identity,
            bias=self.cm[:, 3:4]), waits=p.dma_marks("sync"))
        p.op("scalar", lambda: nc.scalar.activation(
            out=self.qb[:, 1:2], in_=self.qb[:, 0:1], func=AF.Identity,
            scale=np.float32(-1.0)), inc=ss)

        es = self.s1_es = contextlib.ExitStack()
        sb = lambda n, s, d: es.enter_context(nc.sbuf_tensor(n, s, d)).ap()
        self.ps_t = es.enter_context(
            nc.psum_tensor("s1_ps_t", [128, 2048], BF16)).ap()
        self.ps_w = es.enter_context(
            nc.psum_tensor("s1_ps_w", [128, 2048], F32)).ap()
        self.ps_w2 = es.enter_context(
            nc.psum_tensor("s1_ps_w2", [128, 1024], F32)).ap()
        self.aWxT = sb("aWxT", [128, 16 * A_DI], BF16)
        self.aWzT = sb("aWzT", [128, 16 * A_DI], BF16)
        self.aWpT = sb("aWpT", [128, 4 * 1152], BF16)
        self.aWdtT = sb("aWdtT", [128, A_H], BF16)
        self.aWoT = sb("aWoT", [128, 4 * D], BF16)
        self.a_A = sb("a_A", [128, 4], F32)
        self.a_D = sb("a_D", [128, 4], F32)
        self.a_bdt = sb("a_bdt", [128, 1], F32)
        self.a_convw = sb("a_convw", [128, 16], F32)
        self.a_convb = sb("a_convb", [128, 4], F32)
        pa_tmp = contextlib.ExitStack()
        wload = pa_tmp.enter_context(nc.sbuf_tensor("pa_wload", [128, D],
                                                    F32)).ap()
        wbf = pa_tmp.enter_context(nc.sbuf_tensor("pa_wbf", [128, D],
                                                  BF16)).ap()

        self.prep_transposed_sb(self.aWx, A_DI, D, self.aWxT, A_DI, wload, wbf)
        self.prep_transposed_sb(self.aWz, A_DI, D, self.aWzT, A_DI, wload, wbf)
        self.prep_transposed_sb(self.aWp, 1152, A_DI, self.aWpT, 1152,
                                wload, wbf)
        self.prep_transposed_sb(self.aWo, D, A_DI, self.aWoT, D, wload, wbf)
        self.prep_transposed_sb(self.bWdti, B_DTR, D, self.dtinT, B_DTR,
                                wload, wbf)
        self.prep_transposed_sb(self.bWdtp, B_DI, B_DTR, self.dtpjT, B_DI,
                                wload, wbf)
        lv = p.dma("sync", wload[:32, :128], self.aWdt,
                   waits=((ss, self._pt_cast),))
        p.op("scalar", lambda: nc.scalar.activation(
            out=wbf[:32, :128], in_=wload[:32, :128], func=AF.Copy),
            waits=(lv, (sm, self._pt_tr)), inc=ss)
        scv = ss.n
        p.op("tensor", lambda: nc.tensor.transpose(
            self.ps_t[:, 0:32], wbf[:32, :128], self.eye[:32, :32]),
            waits=((ss, scv),), inc=sm)
        mmv = sm.n
        p.op("vector", lambda: nc.vector.tensor_copy(
            self.aWdtT[:, :32], self.ps_t[:, 0:32]), waits=((sm, mmv),),
            inc=sv)
        p.op("tensor", lambda: None, waits=((sv, sv.n),))

        slow = dict(allow_slow_non_contiguous=True)
        lva = p.dma("sync", self.a_A[:, :],
                    self.aAlog.rearrange("h n -> (h n)").rearrange(
                        "(j p) -> p j", p=128), **slow)
        p.op("scalar", lambda: nc.scalar.activation(
            out=self.a_A[:, :], in_=self.a_A[:, :], func=AF.Exp),
            waits=(lva,), inc=ss)
        p.op("vector", lambda: nc.vector.tensor_scalar(
            out=self.a_A[:, :], in0=self.a_A[:, :], scalar1=-1.0,
            scalar2=None, op0=ALU.mult), waits=((ss, ss.n),))
        p.dma("sync", self.a_D[:, :],
              self.aD.rearrange("(j p) -> p j", p=128), **slow)
        p.dma("sync", self.a_bdt[:32, 0:1],
              self.aBdt.rearrange("(h o) -> h o", o=1))
        p.dma("sync", self.a_convw[:, :].rearrange("p (j k) -> p j k", j=4),
              self.aCw.rearrange("c o k -> (c o) k").rearrange("(j p) k -> p j k", p=128), **slow)
        p.dma("sync", self.a_convb[:, :],
              self.aCb.rearrange("(j p) -> p j", p=128), **slow)
        lvab = p.dma("sync",
                     self.b_Ab[:, :].rearrange("p (i n) -> p i n", i=32),
                     self.bAlog.rearrange("(i p) n -> p i n", p=128), **slow)
        p.op("scalar", lambda: nc.scalar.activation(
            out=self.b_Ab[:, :], in_=self.b_Ab[:, :], func=AF.Exp),
            waits=(lvab,), inc=ss)
        p.op("vector", lambda: nc.vector.tensor_scalar(
            out=self.b_Ab[:, :], in0=self.b_Ab[:, :], scalar1=-1.0,
            scalar2=None, op0=ALU.mult), waits=((ss, ss.n),), inc=sv)
        p.dma("sync", self.b_misc[:, 0:32],
              self.bD.rearrange("(i p) -> p i", p=128), **slow)
        p.dma("sync", self.b_misc[:, 32:64],
              self.bCb.rearrange("(i p) -> p i", p=128), **slow)
        p.dma("sync", self.b_misc[:, 64:96],
              self.bBdtp.rearrange("(i p) -> p i", p=128), **slow)
        p.dma("sync", self.b_convw[:, :].rearrange("p (i k) -> p i k", i=32),
              self.bCw.rearrange("(i p) k -> p i k", p=128), **slow)
        pa_tmp.close()

    def s1_attn_a(self):
        p, nc = self.p, self.nc
        sv, ss, sm = p.sem("sv"), p.sem("ss"), p.sem("sm")
        ld, lg = p.sem("ld"), p.sem("lg")
        eye, ps_t, ps_w, ps_w2 = self.eye, self.ps_t, self.ps_w, self.ps_w2
        es = self.s1_es
        sb = lambda n, s, d: es.enter_context(nc.sbuf_tensor(n, s, d)).ap()
        TC = 512
        NCH = TPC // TC

        xt = sb("a_xt", [128, D], F32)
        dump = sb("a_dump", [128, D], BF16)
        st = sb("a_st", [128, 8], F32)
        lnb = sb("a_lnb", [128, D], BF16)
        lnT = sb("a_lnT", [128, 16 * TC], BF16)
        xaT = sb("a_xaT", [128, 4 * (TC + 3)], BF16)
        szT = sb("a_szT", [128, 4 * TC], BF16)
        xactT = sb("a_xactT", [128, 4 * TC], F32)
        xactTb = sb("a_xactTb", [128, 4 * TC], BF16)
        dtfT = sb("a_dtfT", [128, TC], BF16)
        BrT = sb("a_BrT", [128, 4 * TC], F32)
        CrT = sb("a_CrT", [128, 4 * TC], F32)
        deltaT = sb("a_deltaT", [128, TC], F32)
        sptmp = sb("a_sptmp", [128, TC], F32)
        abuf = sb("a_abuf", [128, 4 * TC], F32)
        hbuf = sb("a_hbuf", [128, 4 * TC], F32)
        bbuf = sb("a_bbuf", [128, 4 * TC], F32)
        carry = sb("a_carry", [128, 4], F32)
        y3T = sb("a_y3T", [128, 4 * TC], BF16)
        diag = sb("a_diag", [128, 16 * 128], BF16)
        xres = sb("a_xres", [128, D], F32)

        for ct in range(4):
            for k in range(4):
                def fn(ct=ct, k=k):
                    return nc.vector.tensor_scalar(
                        out=diag[:, (ct * 4 + k) * 128:(ct * 4 + k + 1) * 128],
                        in0=eye[:, :],
                        scalar1=self.a_convw[:, ct * 4 + k:ct * 4 + k + 1],
                        scalar2=None, op0=ALU.mult)
                p.op("vector", fn,
                     inc=(sv if (ct == 3 and k == 3) else None))
        diag_v = sv.n

        for ch in range(NCH):
            bi, ci = divmod(ch, 4)
            for tt in range(4):
                row0 = ch * TC + tt * 128
                lv = p.dma("sync", xt[:, :], self.hs[row0:row0 + 128, :],
                           waits=((ss, getattr(self, "_a_norm", 0)),))
                rdy = ln_stats(p, xt[:, :], D, st, dump,
                               self.cm[:, 0:1], self.cm[:, 2:3],
                               waits=(lv,))
                p.op("scalar", lambda: nc.scalar.activation(
                    out=lnb[:, :], in_=xt[:, :], func=AF.Identity,
                    scale=st[:, 2:3], bias=st[:, 3:4]),
                    waits=((sv, rdy), (sm, getattr(self, "_a_lntr", 0))),
                    inc=ss)
                self._a_norm = ss.n
                scv = ss.n
                for dk0 in (0, 8):
                    for j in range(8):
                        def fn(dk0=dk0, j=j):
                            return nc.tensor.transpose(
                                ps_t[:, j * 128:(j + 1) * 128],
                                lnb[:, (dk0 + j) * 128:(dk0 + j + 1) * 128],
                                eye[:])
                        p.op("tensor", fn, waits=((ss, scv),),
                             inc=(sm if j == 7 else None))
                    mmv = sm.n
                    self._a_lntr = mmv
                    for j in range(8):
                        def ev(dk0=dk0, j=j, tt=tt):
                            return nc.vector.tensor_copy(
                                lnT[:, (dk0 + j) * TC + tt * 128:
                                    (dk0 + j) * TC + (tt + 1) * 128],
                                ps_t[:, j * 128:(j + 1) * 128])
                        p.op("vector", ev, waits=((sm, mmv),),
                             inc=(sv if j == 7 else None))
                    p.op("tensor", lambda: None, waits=((sv, sv.n),))

            if ci == 0:
                for ct in range(4):
                    p.op("vector", lambda ct=ct: nc.vector.memset(
                        xaT[:, ct * (TC + 3):ct * (TC + 3) + 3], 0),
                        waits=((sm, getattr(self, "_a_convmm", 0)),),
                        inc=(sv if ct == 3 else None))
            else:
                for ct in range(4):
                    def cp(ct=ct):
                        return nc.vector.tensor_copy(
                            xaT[:, ct * (TC + 3):ct * (TC + 3) + 3],
                            xaT[:, ct * (TC + 3) + TC:ct * (TC + 3) + TC + 3])
                    p.op("vector", cp,
                         waits=((sm, getattr(self, "_a_convmm", 0)),),
                         inc=(sv if ct == 3 else None))
            pad_done = sv.n

            for ct in range(4):
                for dk in range(16):
                    def fn(ct=ct, dk=dk):
                        return nc.tensor.matmul(
                            ps_w[:, ct * 512:ct * 512 + 512],
                            self.aWxT[:, dk * A_DI + ct * 128:
                                      dk * A_DI + (ct + 1) * 128],
                            lnT[:, dk * TC:(dk + 1) * TC],
                            start=(dk == 0), stop=(dk == 15))
                    p.op("tensor", fn, inc=(sm if dk == 15 else None))
            mmx = sm.n
            for ct in range(4):
                def ev(ct=ct):
                    return nc.scalar.activation(
                        out=xaT[:, ct * (TC + 3) + 3:ct * (TC + 3) + 3 + TC],
                        in_=ps_w[:, ct * 512:(ct + 1) * 512], func=AF.Copy)
                p.op("scalar", ev, waits=((sm, mmx), (sv, pad_done)),
                     inc=(ss if ct == 3 else None))
            evx = ss.n
            p.op("tensor", lambda: None, waits=((ss, evx),))
            for ct in range(4):
                for dk in range(16):
                    def fn(ct=ct, dk=dk):
                        return nc.tensor.matmul(
                            ps_w[:, ct * 512:ct * 512 + 512],
                            self.aWzT[:, dk * A_DI + ct * 128:
                                      dk * A_DI + (ct + 1) * 128],
                            lnT[:, dk * TC:(dk + 1) * TC],
                            start=(dk == 0), stop=(dk == 15))
                    p.op("tensor", fn, inc=(sm if dk == 15 else None))
            mmz = sm.n
            for ct in range(4):
                def ev(ct=ct):
                    return nc.scalar.activation(
                        out=szT[:, ct * TC:(ct + 1) * TC],
                        in_=ps_w[:, ct * 512:(ct + 1) * 512], func=AF.Silu)
                p.op("scalar", ev,
                     waits=((sm, mmz), (sv, getattr(self, "_a_scn", 0))),
                     inc=(ss if ct == 3 else None))
            evz = ss.n
            p.op("tensor", lambda: None, waits=((ss, evz),))

            for ct in range(4):
                for k in range(4):
                    def fn(ct=ct, k=k):
                        return nc.tensor.matmul(
                            ps_w[:, ct * 512:(ct + 1) * 512],
                            diag[:, (ct * 4 + k) * 128:(ct * 4 + k + 1) * 128],
                            xaT[:, ct * (TC + 3) + k:ct * (TC + 3) + k + TC],
                            start=(k == 0), stop=(k == 3))
                    p.op("tensor", fn, waits=((sv, diag_v),),
                         inc=(sm if k == 3 else None))
            mmc = sm.n
            self._a_convmm = mmc
            for ct in range(4):
                def ev(ct=ct):
                    return nc.scalar.activation(
                        out=xactT[:, ct * TC:(ct + 1) * TC],
                        in_=ps_w[:, ct * 512:(ct + 1) * 512], func=AF.Silu,
                        bias=self.a_convb[:, ct:ct + 1])
                p.op("scalar", ev, waits=((sm, mmc),), inc=ss)
            evc = ss.n
            p.op("vector", lambda: nc.vector.tensor_copy(
                xactTb[:, :], xactT[:, :]),
                waits=((ss, evc), (sm, getattr(self, "_a_pmm", 0))), inc=sv)
            evcb = sv.n
            p.op("tensor", lambda: None, waits=((sv, evcb),))

            for fi in range(9):
                for ck in range(4):
                    def fn(fi=fi, ck=ck):
                        return nc.tensor.matmul(
                            ps_w[:, (fi % 4) * 512:(fi % 4) * 512 + 512],
                            self.aWpT[:, ck * 1152 + fi * 128:
                                      ck * 1152 + (fi + 1) * 128],
                            xactTb[:, ck * TC:(ck + 1) * TC],
                            start=(ck == 0), stop=(ck == 3))
                    p.op("tensor", fn, inc=(sm if ck == 3 else None))
                mmp = sm.n
                if fi == 8:
                    self._a_pmm = mmp
                if fi == 0:
                    def ev():
                        return nc.scalar.activation(
                            out=dtfT[:, :], in_=ps_w[:, 0:512], func=AF.Copy)
                elif fi <= 4:
                    def ev(fi=fi):
                        return nc.scalar.activation(
                            out=BrT[:, (fi - 1) * TC:fi * TC],
                            in_=ps_w[:, (fi % 4) * 512:(fi % 4) * 512 + 512],
                            func=AF.Copy)
                else:
                    def ev(fi=fi):
                        return nc.scalar.activation(
                            out=CrT[:, (fi - 5) * TC:(fi - 4) * TC],
                            in_=ps_w[:, (fi % 4) * 512:(fi % 4) * 512 + 512],
                            func=AF.Copy)
                p.op("scalar", ev, waits=((sm, mmp),), inc=ss)
                evp = ss.n
                p.op("tensor", lambda: None, waits=((ss, evp),))

            p.op("tensor", lambda: nc.tensor.matmul(
                ps_w2[:32, 0:512], self.aWdtT[:, :32], dtfT[:, :],
                start=True, stop=True), inc=sm)
            mmd = sm.n
            rdy = softplus(p, ps_w2[:32, 0:512], self.a_bdt[:32, 0:1],
                           deltaT[:32, :], sptmp[:32, :],
                           waits=((sm, mmd),))
            p.op("tensor", lambda: None, waits=((sv, rdy),))

            for ct in range(4):
                psr = ps_w2[:, 0:512] if ct % 2 == 0 else ps_w2[:, 512:1024]

                def fn(ct=ct, psr=psr):
                    return nc.tensor.matmul(
                        psr, self.md[:, ct * 128:(ct + 1) * 128],
                        deltaT[:32, :], start=True, stop=True)
                p.op("tensor", fn, inc=sm)
                mmr = sm.n

                def fa(ct=ct, psr=psr):
                    return nc.scalar.activation(
                        out=abuf[:, ct * TC:(ct + 1) * TC], in_=psr,
                        func=AF.Exp, scale=self.a_A[:, ct:ct + 1])
                p.op("scalar", fa, waits=((sm, mmr),), inc=ss)
                fav = ss.n

                def fb1(ct=ct):
                    return nc.vector.tensor_mul(
                        bbuf[:, ct * TC:(ct + 1) * TC],
                        BrT[:, ct * TC:(ct + 1) * TC],
                        xactT[:, ct * TC:(ct + 1) * TC])
                p.op("vector", fb1, waits=((sm, mmr),))

                def fb2(ct=ct, psr=psr):
                    return nc.vector.tensor_mul(
                        hbuf[:, ct * TC:(ct + 1) * TC],
                        bbuf[:, ct * TC:(ct + 1) * TC], psr)
                p.op("vector", fb2, waits=((ss, fav),), inc=sv)
                p.op("tensor", lambda: None, waits=((sv, sv.n), (ss, ss.n)))
            aexp = ss.n

            for ct in range(4):
                init = 0.0 if ci == 0 else carry[:, ct:ct + 1]

                def fs(ct=ct, init=init):
                    return nc.vector.tensor_tensor_scan(
                        out=bbuf[:, ct * TC:(ct + 1) * TC],
                        data0=abuf[:, ct * TC:(ct + 1) * TC],
                        data1=hbuf[:, ct * TC:(ct + 1) * TC],
                        initial=init, op0=ALU.mult, op1=ALU.add)
                p.op("vector", fs, waits=((ss, aexp),))

                def fc(ct=ct):
                    return nc.vector.tensor_copy(
                        carry[:, ct:ct + 1],
                        bbuf[:, ct * TC + TC - 1:ct * TC + TC])
                p.op("vector", fc)

                def fy(ct=ct):
                    return nc.vector.tensor_mul(
                        hbuf[:, ct * TC:(ct + 1) * TC],
                        bbuf[:, ct * TC:(ct + 1) * TC],
                        CrT[:, ct * TC:(ct + 1) * TC])
                p.op("vector", fy)

                def fy2(ct=ct):
                    return nc.vector.scalar_tensor_tensor(
                        out=bbuf[:, ct * TC:(ct + 1) * TC],
                        in0=xactT[:, ct * TC:(ct + 1) * TC],
                        scalar=self.a_D[:, ct:ct + 1],
                        in1=hbuf[:, ct * TC:(ct + 1) * TC],
                        op0=ALU.mult, op1=ALU.add)
                p.op("vector", fy2)

                def fy3(ct=ct):
                    return nc.vector.tensor_mul(
                        y3T[:, ct * TC:(ct + 1) * TC],
                        bbuf[:, ct * TC:(ct + 1) * TC],
                        szT[:, ct * TC:(ct + 1) * TC])
                p.op("vector", fy3, inc=(sv if ct == 3 else None))
            scn = sv.n
            self._a_scn = scn

            if ch == 0 and "dbg" in self.debug:
                for (off, buf, wid) in ((0, deltaT, TC), (512, abuf, 4 * TC),
                                        (2560, xactT, 4 * TC),
                                        (4608, szT, 4 * TC),
                                        (6656, hbuf, 4 * TC),
                                        (8704, CrT, 4 * TC)):
                    p.dma("gpsimd", self.dbg[:, off:off + wid],
                          buf[:, 0:wid], waits=((sv, scn),))
                p.dma("gpsimd", self.dbg2[:, 0:2048], y3T[:, 0:2048],
                      waits=((sv, scn),))
                p.dma("gpsimd", self.dbg2[:, 2048:4096],
                      self.aWoT[:, 0:2048], waits=((sv, scn),))

            for tt in range(4):
                row0 = ch * TC + tt * 128
                lv = p.dma("sync", xres[:, :], self.hs[row0:row0 + 128, :],
                           waits=((sv, getattr(self, "_a_add", 0)),
                                  getattr(self, "_a_st", None) or (sv, 0)))
                for ni in range(4):
                    for ck in range(4):
                        def fn(tt=tt, ni=ni, ck=ck):
                            return nc.tensor.matmul(
                                ps_w[:, ni * 512:(ni + 1) * 512],
                                y3T[:, ck * TC + tt * 128:
                                    ck * TC + (tt + 1) * 128],
                                self.aWoT[:, ck * D + ni * 512:
                                          ck * D + (ni + 1) * 512],
                                start=(ck == 0), stop=(ck == 3))
                        p.op("tensor", fn, waits=((sv, scn),),
                             inc=(sm if (ni == 3 and ck == 3) else None))
                mmo = sm.n

                def fadd():
                    return nc.vector.tensor_add(xres[:, :], ps_w[:, :],
                                                xres[:, :])
                p.op("vector", fadd, waits=((sm, mmo), lv), inc=sv)
                self._a_add = sv.n
                p.op("tensor", lambda: None, waits=((sv, sv.n),))
                self._a_st = p.dma("gpsimd", self.x1b[row0:row0 + 128, :],
                                   xres[:, :], waits=((sv, self._a_add),))

        self.s1_es.close()

    def prologue_ffn(self):
        p, nc = self.p, self.nc
        sv, ss, sm, ld, lg = (p.sem("sv"), p.sem("ss"), p.sem("sm"),
                              p.sem("ld"), p.sem("lg"))
        es = contextlib.ExitStack()
        sb = lambda n, s, d: es.enter_context(nc.sbuf_tensor(n, s, d)).ap()
        self.ps_t = es.enter_context(
            nc.psum_tensor("pf_ps_t", [128, 1024], BF16)).ap()
        self.ps_w2 = es.enter_context(
            nc.psum_tensor("pf_ps_w2", [128, 512], F32)).ap()
        wl = [sb("pf_wl0", [128, D], F32), sb("pf_wl1", [128, D], F32)]
        t1 = sb("pf_t1", [128, D], F32)
        t2 = sb("pf_t2", [128, D], F32)
        sgn = sb("pf_sgn", [128, D], BF16)
        tstage = sb("pf_ts", [128, D], BF16)
        acc = sb("pf_acc", [128, 2], F32)
        red = sb("pf_red", [128, 1], F32)
        p.op("vector", lambda: nc.vector.memset(acc[:, :], 0), inc=sv)

        def gamma_pass(w_src, nrow, col, subw):
            nsub = subw // D
            m0 = p.flush("vector")
            lvp = [m0, m0]
            cvp = 0
            for fi in range(nrow // 128):
                for s_ in range(nsub):
                    slot = (fi * nsub + s_) % 2
                    lv = p.dma("sync", wl[slot][:, :],
                               w_src[fi * 128:(fi + 1) * 128,
                                     s_ * D:(s_ + 1) * D],
                               waits=(lvp[slot],))

                    def fr(slot=slot):
                        return nc.vector.reduce_sum(
                            red[:, :], wl[slot][:, :], axis=AX.X,
                            apply_absolute_value=True)
                    p.op("vector", fr, waits=(lv,))

                    def fa(col=col):
                        return nc.vector.tensor_add(
                            acc[:, col:col + 1], acc[:, col:col + 1],
                            red[:, :])
                    cvp = p.op("vector", fa, inc=sv)
                    lvp[slot] = (sv, cvp)
        gamma_pass(self.fWfc, FF, 0, D)
        gamma_pass(self.fWpj, D, 1, FF)
        gv = sv.n
        p.op("tensor", lambda: nc.tensor.matmul(
            self.ps_w2[:1, 0:2], self.cm[:, 5:6], acc[:, :],
            start=True, stop=True), waits=((sv, gv),), inc=sm)
        mmv = sm.n
        p.op("scalar", lambda: nc.scalar.activation(
            out=self.gam[0:1, 6:8], in_=self.ps_w2[:1, 0:2], func=AF.Copy,
            scale=np.float32(1.0 / (FF * D))), waits=((sm, mmv),), inc=ss)
        scv = ss.n
        p.op("tensor", lambda: nc.tensor.matmul(
            self.ps_w2[:, 0:2], self.onesr[:, :], self.gam[0:1, 6:8],
            start=True, stop=True), waits=((ss, scv),), inc=sm)
        mmv = sm.n
        p.op("vector", lambda: nc.vector.tensor_copy(
            self.gam[:, 0:2], self.ps_w2[:, 0:2]), waits=((sm, mmv),), inc=sv)
        gv = sv.n
        for col in (0, 1):
            p.op("scalar", lambda col=col: nc.scalar.activation(
                out=self.gam[:, 2 + 2 * col:3 + 2 * col],
                in_=self.gam[:, col:col + 1], func=AF.Identity,
                scale=np.float32(0.5), bias=self.cm[:, 4:5]),
                waits=((sv, gv),))
            p.op("scalar", lambda col=col: nc.scalar.activation(
                out=self.gam[:, 3 + 2 * col:4 + 2 * col],
                in_=self.gam[:, 2 + 2 * col:3 + 2 * col], func=AF.Identity,
                scale=np.float32(-1.0)), inc=ss)
        thv = ss.n
        p.op("vector", lambda: None, waits=((ss, thv),))
        p.op("tensor", lambda: None, waits=((ss, thv),))

        def sign_pass(w_src, nrow, subw, thr_col, out_fn):
            nsub = subw // D
            m0 = p.flush("vector")
            lvp = [m0, m0]
            sgv = 0
            stv = p.dma_marks("gpsimd")
            trv = 0
            for fi in range(nrow // 128):
                for s_ in range(nsub):
                    slot = (fi * nsub + s_) % 2
                    lv = p.dma("sync", wl[slot][:, :],
                               w_src[fi * 128:(fi + 1) * 128,
                                     s_ * D:(s_ + 1) * D],
                               waits=(lvp[slot],))

                    def f1(slot=slot):
                        return nc.vector.tensor_scalar(
                            out=t1[:, :], in0=wl[slot][:, :],
                            scalar1=self.gam[:, thr_col:thr_col + 1],
                            scalar2=None, op0=ALU.is_gt)
                    p.op("vector", f1, waits=(lv,))

                    def f2(slot=slot):
                        return nc.vector.tensor_scalar(
                            out=t2[:, :], in0=wl[slot][:, :],
                            scalar1=self.gam[:, thr_col + 1:thr_col + 2],
                            scalar2=None, op0=ALU.is_lt)
                    p.op("vector", f2, inc=sv)
                    lvp[slot] = (sv, sv.n)
                    p.op("vector", lambda: nc.vector.tensor_sub(
                        sgn[:, :], t1[:, :], t2[:, :]),
                        waits=((sm, trv),), inc=sv)
                    sgv = sv.n
                    for g0 in (0, 8):
                        for j in range(8):
                            def ft(g0=g0, j=j):
                                return nc.tensor.transpose(
                                    self.ps_t[:, j * 128:(j + 1) * 128],
                                    sgn[:, (g0 + j) * 128:(g0 + j + 1) * 128],
                                    self.eye[:])
                            p.op("tensor", ft, waits=((sv, sgv),),
                                 inc=(sm if j == 7 else None))
                        mmv = sm.n
                        trv = mmv

                        def fe(g0=g0):
                            return nc.vector.tensor_copy(
                                tstage[:, g0 * 128:(g0 + 8) * 128],
                                self.ps_t[:, 0:1024])
                        p.op("vector", fe,
                             waits=((sm, mmv),) + tuple(stv), inc=sv)
                        p.op("tensor", lambda: None, waits=((sv, sv.n),))
                    stv = (p.dma("gpsimd", out_fn(fi, s_),
                                 tstage[:, :].rearrange(
                                     "p (dk t) -> p dk t", dk=16),
                                 waits=((sv, sv.n),)),)

        def fc_out(fi, s_):
            q, r = divmod(fi, 16)
            return self.sp_fcsT[q, :, :, r * 128:(r + 1) * 128]
        sign_pass(self.fWfc, FF, D, 2, fc_out)

        def pj_out(mi, fs):
            h, r = divmod(mi, 8)
            return self.sp_pjsT[h, :, fs * 16:(fs + 1) * 16,
                                r * 128:(r + 1) * 128]
        sign_pass(self.fWpj, D, FF, 4, pj_out)
        es.close()

    def ffn(self, x_in, x_out, tag):
        p, nc = self.p, self.nc
        sv, ss, sm, ld, lg = (p.sem("sv"), p.sem("ss"), p.sem("sm"),
                              p.sem("ld"), p.sem("lg"))

        es = contextlib.ExitStack()
        sb = lambda n, s, d: es.enter_context(nc.sbuf_tensor(n, s, d)).ap()
        ps_t = es.enter_context(
            nc.psum_tensor(f"{tag}a_pst", [128, 1024], BF16)).ap()
        xt = [sb(f"{tag}a_xt0", [128, D], F32), sb(f"{tag}a_xt1", [128, D], F32)]
        ln1 = [sb(f"{tag}a_ln10", [128, D], F32),
               sb(f"{tag}a_ln11", [128, D], F32)]
        xn2 = sb(f"{tag}a_xn2", [128, D], F32)
        dump = sb(f"{tag}a_dump", [128, D], BF16)
        st = [sb(f"{tag}a_st0", [128, 16], F32),
              sb(f"{tag}a_st1", [128, 16], F32)]
        xhb = [sb(f"{tag}a_xhb0", [128, D], BF16),
               sb(f"{tag}a_xhb1", [128, D], BF16)]
        tstage = sb(f"{tag}a_ts", [128, D], BF16)
        pv = dict(x=[0, 0], n2=[0, 0], q=[0, 0], tr=0, st=((sv, 0),))
        for tt in range(NTT):
            sl = tt % 2
            lv = p.dma("sync", xt[sl][:, :], x_in[tt * 128:(tt + 1) * 128, :],
                       waits=((ss, pv["x"][sl]),))
            rdy = ln_stats(p, xt[sl][:, :], D, st[sl][:, 0:8], dump,
                           self.cm[:, 0:1], self.cm[:, 2:3],
                           waits=(lv, (sv, pv["n2"][sl])))
            p.op("scalar", lambda sl=sl: nc.scalar.activation(
                out=ln1[sl][:, :], in_=xt[sl][:, :], func=AF.Identity,
                scale=st[sl][:, 2:3], bias=st[sl][:, 3:4]),
                waits=((sv, rdy),), inc=ss)
            pv["x"][sl] = ss.n
            rdy2 = ln_stats(p, ln1[sl][:, :], D, st[sl][:, 8:16], dump,
                            self.cm[:, 1:2], self.cm[:, 2:3],
                            waits=((ss, pv["x"][sl]),))
            p.op("vector", lambda sl=sl: nc.vector.tensor_scalar(
                out=xn2[:, :], in0=ln1[sl][:, :], scalar1=st[sl][:, 10:11],
                scalar2=st[sl][:, 11:12], op0=ALU.mult, op1=ALU.add),
                waits=((sv, rdy2),))
            p.op("vector", lambda sl=sl: nc.vector.reduce_max(
                st[sl][:, 12:13], xn2[:, :], axis=AX.X,
                apply_absolute_value=True))
            p.op("vector", lambda sl=sl: nc.vector.reciprocal(
                st[sl][:, 13:14], st[sl][:, 12:13]))
            p.op("vector", lambda sl=sl: nc.vector.tensor_scalar(
                out=st[sl][:, 13:14], in0=st[sl][:, 13:14], scalar1=128.0,
                scalar2=None, op0=ALU.mult))
            p.op("vector", lambda sl=sl: nc.vector.tensor_scalar(
                out=st[sl][:, 12:13], in0=st[sl][:, 12:13],
                scalar1=float(1.0 / 128.0), scalar2=None, op0=ALU.mult))
            p.op("vector", lambda sl=sl: nc.vector.tensor_scalar(
                out=xn2[:, :], in0=xn2[:, :], scalar1=st[sl][:, 13:14],
                scalar2=self.qb[:, 0:1], op0=ALU.mult, op1=ALU.min),
                inc=sv)
            pv["n2"][sl] = sv.n
            p.op("vector", lambda sl=sl: nc.vector.tensor_scalar(
                out=xhb[sl][:, :], in0=xn2[:, :], scalar1=self.qb[:, 1:2],
                scalar2=st[sl][:, 12:13], op0=ALU.max, op1=ALU.mult),
                waits=((sm, pv["q"][sl]),), inc=sv)
            qv = sv.n
            for g0 in (0, 8):
                for j in range(8):
                    def ft(g0=g0, j=j, sl=sl):
                        return nc.tensor.transpose(
                            ps_t[:, j * 128:(j + 1) * 128],
                            xhb[sl][:, (g0 + j) * 128:(g0 + j + 1) * 128],
                            self.eye[:])
                    p.op("tensor", ft, waits=((sv, qv),),
                         inc=(sm if j == 7 else None))
                mmv = sm.n
                pv["q"][sl] = mmv

                def fe(g0=g0):
                    return nc.vector.tensor_copy(
                        tstage[:, g0 * 128:(g0 + 8) * 128], ps_t[:, 0:1024])
                p.op("vector", fe, waits=((sm, mmv),) + tuple(pv["st"]),
                     inc=sv)
                p.op("tensor", lambda: None, waits=((sv, sv.n),))
            pv["st"] = (p.dma("gpsimd", self.sp_xhT[tt, :, :], tstage[:, :],
                              waits=((sv, sv.n),)),)
        es.close()
        p.fence()

        es = contextlib.ExitStack()
        sb = lambda n, s, d: es.enter_context(nc.sbuf_tensor(n, s, d)).ap()
        psp = [es.enter_context(
                   nc.psum_tensor(f"{tag}b_ps{i}", [128, 2048], F32)).ap()
               for i in range(2)]
        wq = sb(f"{tag}b_wq", [128, 16 * D], BF16)
        xsl = [sb(f"{tag}b_xs0", [128, 16 * 128], BF16),
               sb(f"{tag}b_xs1", [128, 16 * 128], BF16)]
        hst = [sb(f"{tag}b_h0", [128, D], BF16),
               sb(f"{tag}b_h1", [128, D], BF16)]
        xv = [0, 0]
        gv = [0, 0]
        hv = [p.dma_marks("gpsimd"), p.dma_marks("gpsimd")]
        for ncq in range(4):
            wv = p.dma("sync", wq[:, :],
                       self.sp_fcsT[ncq].rearrange("p a b -> p (a b)"),
                       waits=((sm, p.sem("sm").n),))
            for tt in range(NTT):
                slot = tt % 2
                lv = p.dma("sync", xsl[slot][:, :], self.sp_xhT[tt, :, :],
                           waits=((sm, xv[slot]),))
                psc = psp[slot]
                for dk in range(16):
                    for ncc in range(4):
                        def fm(dk=dk, ncc=ncc, slot=slot, psc=psc):
                            return nc.tensor.matmul(
                                psc[:, ncc * 512:(ncc + 1) * 512],
                                xsl[slot][:, dk * 128:(dk + 1) * 128],
                                wq[:, dk * D + ncc * 512:
                                   dk * D + (ncc + 1) * 512],
                                start=(dk == 0), stop=(dk == 15))
                        p.op("tensor", fm,
                             waits=(wv, lv, (ss, gv[slot])) if dk == 0
                             and ncc == 0 else (),
                             inc=(sm if dk == 15 and ncc == 3 else None))
                xv[slot] = sm.n
                mmv = sm.n

                def fg(slot=slot, psc=psc):
                    return nc.scalar.activation(
                        out=hst[slot][:, :], in_=psc[:, :], func=AF.Gelu,
                        scale=self.gam[:, 0:1])
                p.op("scalar", fg,
                     waits=((sm, mmv),) + tuple(hv[slot]), inc=ss)
                gv[slot] = ss.n
                ggv = ss.n
                hv[slot] = (p.dma("gpsimd",
                                  self.sp_h[tt * 128:(tt + 1) * 128,
                                            ncq * D:(ncq + 1) * D],
                                  hst[slot][:, :], waits=((ss, ggv),)),)
        es.close()
        p.fence()

        outer_es = contextlib.ExitStack()
        psq = [outer_es.enter_context(
                   nc.psum_tensor(f"{tag}d_ps{i}", [128, 512], F32)).ap()
               for i in range(4)]
        es = contextlib.ExitStack()
        sb = lambda n, s, d: es.enter_context(nc.sbuf_tensor(n, s, d)).ap()
        ps_t = es.enter_context(
            nc.psum_tensor(f"{tag}c_pst", [128, 1024], BF16)).ap()
        hrow = [sb(f"{tag}c_h0", [128, FF], BF16),
                sb(f"{tag}c_h1", [128, FF], BF16)]
        xn2h = sb(f"{tag}c_xn2", [128, FF], F32)
        st2 = [sb(f"{tag}c_st0", [128, 8], F32),
               sb(f"{tag}c_st1", [128, 8], F32)]
        hhb = sb(f"{tag}c_hhb", [128, FF], BF16)
        hh = [sb(f"{tag}cd_hh0", [128, FF], BF16),
              sb(f"{tag}cd_hh1", [128, FF], BF16)]
        wqd = sb(f"{tag}d_w", [128, 64 * 512], BF16)
        xres = [sb(f"{tag}d_x{i}", [128, 512], F32) for i in range(2)]

        wv = p.dma("sync", wqd[:, :].rearrange("p (a b) -> p a b", a=64),
                   self.sp_pjsT[0, :, :, 0:512],
                   waits=((sm, p.sem("sm").n),))
        st_marks = [None] * NTT
        stq = [((sv, 0),), ((sv, 0),)]
        mm2 = [0, 0]
        ev2 = [0, 0]
        favq = [0, 0, 0, 0]
        favx = [0, 0]
        xv2 = [None, None]
        xo2 = [((sv, 0),), ((sv, 0),)]
        mul_mark = [0, 0]
        rdy_s = [0, 0]
        lvp = [None, None]
        qv_prev = 0
        tr_last = 0

        def stage_ls(t):
            sl = t % 2
            lvp[sl] = p.dma("sync", hrow[sl][:, :],
                            self.sp_h[t * 128:(t + 1) * 128, :],
                            waits=((ss, mul_mark[sl]),))
            w = (lvp[sl], (ss, mul_mark[sl]), (sm, mm2[sl]))
            if t >= 2:
                w = w + (st_marks[t - 2],)
            rdy_s[sl] = ln_stats(p, hrow[sl][:, :], FF, st2[sl], hh[sl],
                                 self.cm[:, 1:2], self.cm[:, 2:3], waits=w)

        for tt in range(NTT):
            sl = tt % 2
            s4 = tt % 4
            if tt == 0:
                stage_ls(0)
            if tt + 1 < NTT:
                stage_ls(tt + 1)
            xv2[sl] = p.dma("sync", xres[sl][:, :],
                            x_in[tt * 128:(tt + 1) * 128, 0:512],
                            waits=((sv, favx[sl]),) + xo2[sl])
            p.op("scalar", lambda sl=sl: nc.scalar.activation(
                out=xn2h[:, :], in_=hrow[sl][:, :], func=AF.Identity,
                scale=st2[sl][:, 2:3], bias=st2[sl][:, 3:4]),
                waits=((sv, rdy_s[sl]), (sv, qv_prev)), inc=ss)
            mul_mark[sl] = ss.n
            p.op("vector", lambda sl=sl: nc.vector.reduce_max(
                st2[sl][:, 5:6], xn2h[:, :], axis=AX.X,
                apply_absolute_value=True), waits=((ss, mul_mark[sl]),))
            p.op("vector", lambda sl=sl: nc.vector.reciprocal(
                st2[sl][:, 6:7], st2[sl][:, 5:6]))
            p.op("vector", lambda sl=sl: nc.vector.tensor_scalar(
                out=st2[sl][:, 6:7], in0=st2[sl][:, 6:7], scalar1=128.0,
                scalar2=None, op0=ALU.mult))
            p.op("vector", lambda sl=sl: nc.vector.tensor_scalar(
                out=st2[sl][:, 5:6], in0=st2[sl][:, 5:6],
                scalar1=float(1.0 / 128.0), scalar2=None, op0=ALU.mult))
            p.op("vector", lambda sl=sl: nc.vector.tensor_scalar(
                out=xn2h[:, :], in0=xn2h[:, :], scalar1=st2[sl][:, 6:7],
                scalar2=self.qb[:, 0:1], op0=ALU.mult, op1=ALU.min))
            p.op("vector", lambda sl=sl: nc.vector.tensor_scalar(
                out=hhb[:, :], in0=xn2h[:, :], scalar1=self.qb[:, 1:2],
                scalar2=st2[sl][:, 5:6], op0=ALU.max, op1=ALU.mult),
                waits=((sm, tr_last),), inc=sv)
            qv_prev = sv.n
            qvv = sv.n
            if tt > 0:
                pq = (tt - 1) % 4
                px = (tt - 1) % 2

                def fadd(pq=pq, px=px):
                    return nc.vector.scalar_tensor_tensor(
                        out=xres[px][:, :], in0=psq[pq][:, :],
                        scalar=self.gam[:, 1:2], in1=xres[px][:, :],
                        op0=ALU.mult, op1=ALU.add)
                p.op("vector", fadd, waits=((sm, mm2[px]), xv2[px]),
                     inc=sv)
                favq[pq] = favx[px] = sv.n
                xo2[px] = (p.dma("gpsimd",
                                 x_out[(tt - 1) * 128:tt * 128, 0:512],
                                 xres[px][:, :], waits=((sv, sv.n),)),)
            for g0 in range(0, 64, 8):
                for j in range(8):
                    def ft(g0=g0, j=j):
                        return nc.tensor.transpose(
                            ps_t[:, j * 128:(j + 1) * 128],
                            hhb[:, (g0 + j) * 128:(g0 + j + 1) * 128],
                            self.eye[:])
                    p.op("tensor", ft, waits=((sv, qvv),),
                         inc=(sm if j == 7 else None))
                mmv = sm.n

                def fe(g0=g0, sl=sl):
                    return nc.vector.tensor_copy(
                        hh[sl][:, g0 * 128:(g0 + 8) * 128], ps_t[:, 0:1024])
                p.op("vector", fe, waits=((sm, mmv),) + tuple(stq[sl]),
                     inc=sv)
                p.op("tensor", lambda: None, waits=((sv, sv.n),))
            tr_last = mmv
            ev2[sl] = sv.n
            st_marks[tt] = p.dma("gpsimd", self.sp_hhT[tt, :, :],
                                 hh[sl][:, :], waits=((sv, ev2[sl]),))
            stq[sl] = (st_marks[tt],)
            for fk in range(64):
                def fm(fk=fk, sl=sl, s4=s4):
                    return nc.tensor.matmul(
                        psq[s4][:, :], hh[sl][:, fk * 128:(fk + 1) * 128],
                        wqd[:, fk * 512:(fk + 1) * 512],
                        start=(fk == 0), stop=(fk == 63))
                p.op("tensor", fm,
                     waits=(wv, (sv, ev2[sl]),
                            (sv, favq[s4])) if fk == 0 else (),
                     inc=(sm if fk == 63 else None))
            mm2[sl] = sm.n
        pq, px = (NTT - 1) % 4, (NTT - 1) % 2
        p.op("vector", lambda pq=pq, px=px: nc.vector.scalar_tensor_tensor(
            out=xres[px][:, :], in0=psq[pq][:, :], scalar=self.gam[:, 1:2],
            in1=xres[px][:, :], op0=ALU.mult, op1=ALU.add),
            waits=((sm, mm2[px]), xv2[px]), inc=sv)
        favq[pq] = favx[px] = sv.n
        p.dma("gpsimd", x_out[(NTT - 1) * 128:NTT * 128, 0:512],
              xres[px][:, :], waits=((sv, sv.n),))
        es.close()
        p.fence()

        tes = contextlib.ExitStack()
        tsb = lambda n, s, d: tes.enter_context(nc.sbuf_tensor(n, s, d)).ap()
        wqt = tsb(f"{tag}t_w", [128, 64 * 1024], BF16)
        hh2 = [tsb(f"{tag}t_h0", [128, FF], BF16),
               tsb(f"{tag}t_h1", [128, FF], BF16)]
        xr2 = [tsb(f"{tag}t_x{i}", [128, 512], F32) for i in range(4)]
        xo4 = [((sv, 0),)] * 4
        xv4 = [None] * 4
        mmt2 = [0, 0]
        mmq4 = [0, 0, 0, 0]

        wv = p.dma("sync",
                   wqt[:, 0:64 * 512].rearrange("p (a b) -> p a b", a=64),
                   self.sp_pjsT[0, :, :, 512:1024],
                   waits=((sm, p.sem("sm").n),))
        for tt in range(NTT):
            sl = tt % 2
            s4 = tt % 4
            lw = ((sm, mmt2[sl]), st_marks[tt])
            lv1 = p.dma("sync", hh2[sl][:, 0:FF // 2],
                        self.sp_hhT[tt, :, 0:FF // 2], waits=lw)
            lv2 = p.dma("gpsimd", hh2[sl][:, FF // 2:],
                        self.sp_hhT[tt, :, FF // 2:], waits=lw)
            xv4[s4] = p.dma("sync", xr2[s4][:, :],
                            x_in[tt * 128:(tt + 1) * 128, 512:1024],
                            waits=((sv, favq[s4]),) + xo4[s4])
            for fk in range(64):
                def fm(fk=fk, sl=sl, s4=s4):
                    return nc.tensor.matmul(
                        psq[s4][:, :], hh2[sl][:, fk * 128:(fk + 1) * 128],
                        wqt[:, fk * 512:(fk + 1) * 512],
                        start=(fk == 0), stop=(fk == 63))
                if fk == 0:
                    w_ = (wv, lv1, (sv, favq[s4]))
                elif fk == 32:
                    w_ = (lv2,)
                else:
                    w_ = ()
                p.op("tensor", fm, waits=w_,
                     inc=(sm if fk == 63 else None))
            mmt2[sl] = mmq4[s4] = sm.n

            def fadd(s4=s4):
                return nc.vector.scalar_tensor_tensor(
                    out=xr2[s4][:, :], in0=psq[s4][:, :],
                    scalar=self.gam[:, 1:2], in1=xr2[s4][:, :],
                    op0=ALU.mult, op1=ALU.add)
            p.op("vector", fadd, waits=((sm, mmq4[s4]), xv4[s4]), inc=sv)
            favq[s4] = sv.n
            xo4[s4] = (p.dma("gpsimd",
                             x_out[tt * 128:(tt + 1) * 128, 512:1024],
                             xr2[s4][:, :], waits=((sv, favq[s4]),)),)

        wv = p.dma("sync",
                   wqt[:, :].rearrange("p (a b) -> p a b", a=64),
                   self.sp_pjsT[1, :, :, :],
                   waits=((sm, p.sem("sm").n),))
        for tt in range(NTT):
            sl = tt % 2
            lw = ((sm, mmt2[sl]), st_marks[tt])
            lv1 = p.dma("sync", hh2[sl][:, 0:FF // 2],
                        self.sp_hhT[tt, :, 0:FF // 2], waits=lw)
            lv2 = p.dma("gpsimd", hh2[sl][:, FF // 2:],
                        self.sp_hhT[tt, :, FF // 2:], waits=lw)
            for ncc in range(2):
                s4 = 2 * sl + ncc
                xv4[s4] = p.dma("sync", xr2[s4][:, :],
                                x_in[tt * 128:(tt + 1) * 128,
                                     (2 + ncc) * 512:(3 + ncc) * 512],
                                waits=((sv, favq[s4]),) + xo4[s4])
            for fk in range(64):
                for ncc in range(2):
                    s4 = 2 * sl + ncc

                    def fm(fk=fk, sl=sl, ncc=ncc, s4=s4):
                        return nc.tensor.matmul(
                            psq[s4][:, :],
                            hh2[sl][:, fk * 128:(fk + 1) * 128],
                            wqt[:, fk * 1024 + ncc * 512:
                                fk * 1024 + (ncc + 1) * 512],
                            start=(fk == 0), stop=(fk == 63))
                    if fk == 0 and ncc == 0:
                        w_ = (wv, lv1, (sv, favq[2 * sl]),
                              (sv, favq[2 * sl + 1]))
                    elif fk == 32 and ncc == 0:
                        w_ = (lv2,)
                    else:
                        w_ = ()
                    p.op("tensor", fm, waits=w_,
                         inc=(sm if fk == 63 and ncc == 1 else None))
            mmt2[sl] = sm.n
            for ncc in range(2):
                s4 = 2 * sl + ncc

                def fadd(s4=s4):
                    return nc.vector.scalar_tensor_tensor(
                        out=xr2[s4][:, :], in0=psq[s4][:, :],
                        scalar=self.gam[:, 1:2], in1=xr2[s4][:, :],
                        op0=ALU.mult, op1=ALU.add)
                p.op("vector", fadd, waits=((sm, mmt2[sl]), xv4[s4]),
                     inc=sv)
                favq[s4] = sv.n
                xo4[s4] = (p.dma("gpsimd",
                                 x_out[tt * 128:(tt + 1) * 128,
                                       (2 + ncc) * 512:(3 + ncc) * 512],
                                 xr2[s4][:, :], waits=((sv, favq[s4]),)),)
        tes.close()
        outer_es.close()
        p.fence()

    def prologue_b(self):
        p, nc = self.p, self.nc
        sv, ss, sm = p.sem("sv"), p.sem("ss"), p.sem("sm")
        es = contextlib.ExitStack()
        sb = lambda n, s, d: es.enter_context(nc.sbuf_tensor(n, s, d)).ap()
        self.ps_t = es.enter_context(
            nc.psum_tensor("pb_ps_t", [128, 1024], BF16)).ap()
        wl = [sb("pb_wl0", [128, D], F32), sb("pb_wl1", [128, D], F32)]
        wbf = sb("pb_wbf", [128, D], BF16)
        tstage = sb("pb_ts", [128, 32 * 128], BF16)

        def prep(w_src, nrow, ncol, out_fn):
            m0 = p.flush("vector")
            lvp = [m0, m0]
            cvp = 0
            trv = 0
            stv = p.dma_marks("gpsimd")
            nsub = ncol // D
            for fi in range(nrow // 128):
                for cs in range(nsub):
                    slot = (fi * nsub + cs) % 2
                    lv = p.dma("sync", wl[slot][:, :],
                               w_src[fi * 128:(fi + 1) * 128,
                                     cs * D:(cs + 1) * D],
                               waits=(lvp[slot],))
                    p.op("scalar", lambda slot=slot: nc.scalar.activation(
                        out=wbf[:, :], in_=wl[slot][:, :], func=AF.Copy),
                        waits=(lv, (sm, trv)), inc=ss)
                    lvp[slot] = (ss, ss.n)
                    scv = ss.n
                    for g0 in (0, 8):
                        for j in range(8):
                            def ft(g0=g0, j=j):
                                return nc.tensor.transpose(
                                    self.ps_t[:, j * 128:(j + 1) * 128],
                                    wbf[:, (g0 + j) * 128:(g0 + j + 1) * 128],
                                    self.eye[:])
                            p.op("tensor", ft, waits=((ss, scv),),
                                 inc=(sm if j == 7 else None))
                        mmv = sm.n
                        trv = mmv

                        def fe(g0=g0, cs=cs):
                            return nc.vector.tensor_copy(
                                tstage[:, (cs * 16 + g0) * 128:
                                       (cs * 16 + g0 + 8) * 128],
                                self.ps_t[:, 0:1024])
                        p.op("vector", fe, waits=((sm, mmv),) + tuple(stv),
                             inc=sv)
                        p.op("tensor", lambda: None, waits=((sv, sv.n),))
                    if cs == nsub - 1:
                        stv = (out_fn(fi),)

        def inp_out(fi):
            return p.dma("gpsimd", self.sp_inpT[fi, :, :, :],
                         tstage[:, 0:2048].rearrange("p (a b) -> p a b", a=16),
                         waits=((sv, p.sem("sv").n),))
        prep(self.bWin, 2 * B_DXB + 2 * B_DI, D, inp_out)

        def wo_out(mi):
            half, r = divmod(mi, 8)
            return p.dma("gpsimd",
                         self.sp_woT[half][:, :, r * 128:(r + 1) * 128],
                         tstage[:, :].rearrange("p (a b) -> p a b", a=32),
                         waits=((sv, p.sem("sv").n),))
        prep(self.bWo, D, B_DI, wo_out)
        es.close()

    def attn_b(self):
        p, nc = self.p, self.nc
        sv, ss, sm = p.sem("sv"), p.sem("ss"), p.sem("sm")
        sp = p.sem("sp")
        y3sp = self.sp_y3.rearrange("(b tb cp ck t) -> b tb cp ck t",
                                    b=B_PER_CORE, tb=16, cp=128, ck=32)
        eso = contextlib.ExitStack()
        dtlT = eso.enter_context(
            nc.sbuf_tensor("b_dtlT", [128, B_PER_CORE * L], BF16)).ap()

        es3 = contextlib.ExitStack()
        sb3 = lambda n, s, d: es3.enter_context(nc.sbuf_tensor(n, s, d)).ap()
        psB = [es3.enter_context(
                   nc.psum_tensor(f"b2_psb{j}", [128, 1024], F32)).ap()
               for j in range(2)]
        psC = [es3.enter_context(
                   nc.psum_tensor(f"b2_psc{j}", [128, 512], F32)).ap()
               for j in range(2)]
        dtT = sb3("b_dtT", [128, L], F32)
        xactT = sb3("b_xactT", [128, L], BF16)
        dtx = sb3("b_dtx", [128, L], BF16)
        abuf = [sb3("b_abuf0", [128, L], F32), sb3("b_abuf1", [128, L], F32)]
        bbuf = sb3("b_bbuf", [128, L], BF16)
        hbuf = sb3("b_hbuf", [128, L], F32)
        mbuf = [sb3("b_mbuf0", [128, L], F32), sb3("b_mbuf1", [128, L], F32)]
        ybuf = sb3("b_ybuf", [128, L], F32)
        xfull = sb3("b_xfull", [128, L + 3], BF16)
        xrows = sb3("b_xrows", [32, L], BF16)
        brows = sb3("b_brows", [32, L], BF16)
        crows = sb3("b_crows", [128, L], BF16)
        szrow = sb3("b_szrow", [128, L], BF16)
        y3b = sb3("b_y3b", [128, L], BF16)
        diag = sb3("b_diag", [128, 4 * 128], BF16)
        sptmp = abuf[1]

        st_scan = dict(
            bmh=[0, 0], scanv=[0, 0], scn_last=0, mh_last=0, mhC=[0, 0],
            ypool=[0, 0], prev_replv=0, prev_bbv=0, prev_ccv=0, prev_y3v=0,
            prev_xav=0, prev_lvx=(sv, 0), prev_lvz=(sv, 0),
            prev_lvb=(sv, 0), prev_lvc=(sv, 0),
            pv_y3st=p.dma_marks("gpsimd"))

        def scan_dtile(b, i, S, fill=None):
            for tc in range(4):
                def fm(tc=tc, i=i, b=b):
                    return nc.tensor.matmul(
                        psB[tc // 2][:, (tc % 2) * 512:(tc % 2 + 1) * 512],
                        self.dtpjT[:, i * 128:(i + 1) * 128],
                        dtlT[:, b * L + tc * 512:b * L + (tc + 1) * 512],
                        start=True, stop=True)
                p.op("tensor", fm,
                     waits=((ss, S["prev_xav"]), (sv, S["bmh"][0]),
                            (sv, S["bmh"][1])) if tc == 0 else (),
                     inc=(sm if tc == 3 else None))
            mmv = sm.n
            p.op("scalar", lambda i=i: nc.scalar.activation(
                out=dtT[:, 0:1024], in_=psB[0][:, :], func=AF.Identity,
                bias=self.b_misc[:, 64 + i:65 + i]),
                waits=((sm, mmv), (sv, S["scanv"][1])))
            p.op("scalar", lambda i=i: nc.scalar.activation(
                out=dtT[:, 1024:2048], in_=psB[1][:, :], func=AF.Identity,
                bias=self.b_misc[:, 64 + i:65 + i]), inc=ss)
            p.op("scalar", lambda: nc.scalar.activation(
                out=sptmp[:, :], in_=dtT[:, :], func=AF.Abs))
            p.op("scalar", lambda: nc.scalar.activation(
                out=sptmp[:, :], in_=sptmp[:, :], func=AF.Exp,
                scale=np.float32(-1.0)))
            p.op("scalar", lambda: nc.scalar.activation(
                out=sptmp[:, :], in_=sptmp[:, :], func=AF.Ln,
                bias=np.float32(1.0)), inc=ss)
            v1 = ss.n
            p.op("vector", lambda: nc.vector.tensor_scalar(
                out=dtT[:, :], in0=dtT[:, :], scalar1=0.0, scalar2=None,
                op0=ALU.max), waits=((ss, v1),))
            p.op("vector", lambda: nc.vector.tensor_add(
                dtT[:, :], dtT[:, :], sptmp[:, :]), inc=sv)
            rdy = sv.n
            lvx = p.dma("sync", xrows[:, :],
                        self.sp_x[b][32 * i:32 * i + 32, :],
                        waits=((sm, S["prev_replv"]), S["prev_lvx"]))
            S["prev_lvx"] = lvx
            for k in range(4):
                def fd(k=k, i=i):
                    return nc.vector.tensor_scalar(
                        out=diag[:, k * 128:(k + 1) * 128],
                        in0=self.eye[:, :],
                        scalar1=self.b_convw[:, 4 * i + k:4 * i + k + 1],
                        scalar2=None, op0=ALU.mult)
                p.op("vector", fd, inc=(sv if k == 3 else None))
            dgv = sv.n
            p.op("tensor", lambda: None, waits=((sv, rdy),))
            for tc in range(4):
                def fm(tc=tc):
                    return nc.tensor.matmul(
                        psB[tc // 2][:, (tc % 2) * 512:(tc % 2 + 1) * 512],
                        self.mx[:, :], xrows[:, tc * 512:(tc + 1) * 512],
                        start=True, stop=True)
                p.op("tensor", fm, waits=(lvx,) if tc == 0 else (),
                     inc=(sm if tc == 3 else None))
            mmv = sm.n
            S["prev_replv"] = mmv
            p.op("vector", lambda mmv=mmv: nc.vector.memset(
                xfull[:, 0:3], 0), waits=((sm, mmv),))
            p.op("scalar", lambda: nc.scalar.activation(
                out=xfull[:, 3:3 + 1024], in_=psB[0][:, :], func=AF.Copy),
                waits=((sm, mmv),))
            p.op("scalar", lambda: nc.scalar.activation(
                out=xfull[:, 3 + 1024:3 + L], in_=psB[1][:, :],
                func=AF.Copy), inc=ss)
            xfv = ss.n
            vfv = p.flush("vector")
            p.op("tensor", lambda: None, waits=((ss, xfv), vfv))
            for k in range(4):
                for tc in range(4):
                    def fm(k=k, tc=tc):
                        return nc.tensor.matmul(
                            psB[tc // 2][:, (tc % 2) * 512:
                                         (tc % 2 + 1) * 512],
                            diag[:, k * 128:(k + 1) * 128],
                            xfull[:, k + tc * 512:k + tc * 512 + 512],
                            start=(k == 0), stop=(k == 3))
                    p.op("tensor", fm, waits=((sv, dgv),) if k == 0
                         and tc == 0 else (),
                         inc=(sm if k == 3 and tc == 3 else None))
            mmv = sm.n
            p.op("scalar", lambda i=i: nc.scalar.activation(
                out=xactT[:, 0:1024], in_=psB[0][:, :], func=AF.Silu,
                bias=self.b_misc[:, 32 + i:33 + i]), waits=((sm, mmv),))
            p.op("scalar", lambda i=i: nc.scalar.activation(
                out=xactT[:, 1024:2048], in_=psB[1][:, :], func=AF.Silu,
                bias=self.b_misc[:, 32 + i:33 + i]), inc=ss)
            xav = ss.n
            S["prev_xav"] = xav
            p.op("vector", lambda: nc.vector.tensor_mul(
                dtx[:, :], dtT[:, :], xactT[:, :]),
                waits=((ss, xav),), inc=sv)
            dxv = sv.n
            p.op("tensor", lambda: None, waits=((sv, dxv),))
            lvz = p.dma("sync", szrow[:, :],
                        self.sp_sz[b][128 * i:128 * (i + 1), :],
                        waits=((sv, S["prev_y3v"]), S["prev_lvz"]))
            S["prev_lvz"] = lvz
            lvb = p.dma("sync", brows[:, :],
                        self.sp_B[b][32 * i:32 * i + 32, :],
                        waits=((sm, S["prev_bbv"]), S["prev_lvb"]))
            S["prev_lvb"] = lvb
            lvc = p.dma("sync", crows[:, :],
                        self.sp_C[b][128 * i:128 * (i + 1), :],
                        waits=((sm, S["prev_ccv"]), S["prev_lvc"]))
            S["prev_lvc"] = lvc

            for n in range(16):
                asl = n % 2

                def fa(i=i, n=n, asl=asl):
                    return nc.scalar.activation(
                        out=abuf[asl][:, :], in_=dtT[:, :], func=AF.Exp,
                        scale=self.b_Ab[:, 16 * i + n:16 * i + n + 1])
                p.op("scalar", fa, waits=((sv, S["scanv"][asl]),), inc=ss)
                aev = ss.n
                mmbh = [0, 0]
                for bh in range(2):
                    for tc in range(2):
                        def fb_(n=n, bh=bh, tc=tc):
                            return nc.tensor.matmul(
                                psB[bh][:, tc * 512:(tc + 1) * 512],
                                self.mb[:, n * 128:(n + 1) * 128],
                                brows[:, bh * 1024 + tc * 512:
                                      bh * 1024 + (tc + 1) * 512],
                                start=True, stop=True)
                        p.op("tensor", fb_,
                             waits=((sv, S["bmh"][bh]), lvb, (ss, xav))
                             if tc == 0 else (),
                             inc=(sm if tc == 1 else None))
                    mmbh[bh] = sm.n
                S["prev_bbv"] = mmbh[1]
                for bh in range(2):
                    def fb(n=n, bh=bh):
                        return nc.vector.tensor_mul(
                            bbuf[:, bh * 1024:(bh + 1) * 1024],
                            dtx[:, bh * 1024:(bh + 1) * 1024],
                            psB[bh][:, :])
                    p.op("vector", fb,
                         waits=((sm, mmbh[bh]), (sv, S["scn_last"])), inc=sv)
                    S["bmh"][bh] = sv.n
                if fill is not None:
                    next(fill, None)

                def fs(asl=asl):
                    return nc.vector.tensor_tensor_scan(
                        out=hbuf[:, :], data0=abuf[asl][:, :],
                        data1=bbuf[:, :], initial=0.0,
                        op0=ALU.mult, op1=ALU.add)
                p.op("vector", fs,
                     waits=((ss, aev), (sv, S["mh_last"])), inc=sv)
                S["scanv"][asl] = sv.n
                S["scn_last"] = sv.n
                msl = n % 2
                tgt = ybuf if n == 0 else mbuf[msl]
                for q in range(4):
                    cs = q % 2

                    def fcm(n=n, q=q, cs=cs):
                        return nc.tensor.matmul(
                            psC[cs][:, :],
                            self.mc[:, n * 128:(n + 1) * 128],
                            crows[:, q * 512:(q + 1) * 512],
                            start=True, stop=True)
                    p.op("tensor", fcm,
                         waits=((sv, S["mhC"][cs]), lvc), inc=sm)
                    mmc = sm.n
                    S["prev_ccv"] = mmc

                    def fmm(q=q, cs=cs, tgt=tgt):
                        return nc.vector.tensor_mul(
                            tgt[:, q * 512:(q + 1) * 512],
                            hbuf[:, q * 512:(q + 1) * 512],
                            psC[cs][:, :])
                    p.op("vector", fmm,
                         waits=((sm, mmc), (sp, S["ypool"][msl])) if n > 0
                         else ((sm, mmc), (sp, p.sem("sp").n)),
                         inc=sv)
                    S["mhC"][cs] = sv.n
                S["mh_last"] = sv.n
                if n > 0:
                    def fpool(msl=msl):
                        return nc.gpsimd.tensor_add(
                            ybuf[:, :], ybuf[:, :], mbuf[msl][:, :])
                    p.op("gpsimd", fpool, waits=((sv, S["mhC"][1]),), inc=sp)
                    S["ypool"][msl] = sp.n
            p.op("vector", lambda i=i: nc.vector.scalar_tensor_tensor(
                out=mbuf[0][:, :], in0=xactT[:, :],
                scalar=self.b_misc[:, i:i + 1], in1=ybuf[:, :],
                op0=ALU.mult, op1=ALU.add), waits=((sp, sp.n),))
            p.op("vector", lambda: nc.vector.tensor_mul(
                y3b[:, :], mbuf[0][:, :], szrow[:, :]),
                waits=(lvz,) + tuple(S["pv_y3st"]), inc=sv)
            y3v = sv.n
            S["prev_y3v"] = y3v
            S["pv_y3st"] = (p.dma(
                "gpsimd", y3sp[b][:, :, i, :].rearrange("tb cp t -> cp tb t"),
                y3b[:, :].rearrange("p (tb t) -> p tb t", tb=16),
                waits=((sv, y3v),)),)

        es1 = contextlib.ExitStack()
        sb = lambda n, s, d: es1.enter_context(nc.sbuf_tensor(n, s, d)).ap()
        psI = es1.enter_context(
            nc.psum_tensor("b1_psI", [128, 512], F32)).ap()
        lnT = sb("b_lnT", [128, 16 * L], BF16)
        wsl = sb("b_w0", [128, 16 * 128], BF16)
        stg = [sb("b_stg0", [128, L], BF16), sb("b_stg1", [128, L], BF16)]
        st_ip = dict(wv=None, egv=0, sgv=[p.dma_marks("gpsimd"),
                                          p.dma_marks("gpsimd")],
                     prev_lnv=0, evv=[0, 0])

        def emit_ln(b):
            esLN = contextlib.ExitStack()
            ps_t = esLN.enter_context(
                nc.psum_tensor(f"b1_pst_{b}", [128, 1024], BF16)).ap()
            xt = mbuf[0]
            dump = y3b
            stt = esLN.enter_context(
                nc.sbuf_tensor(f"b1_st_{b}", [128, 8], F32)).ap()
            lnb = xfull[:, 0:D]
            pnorm = 0
            ptr = 0
            r0 = b * L
            for tt in range(16):
                row0 = r0 + tt * 128
                lv = p.dma("sync", xt[:, :], self.x2b[row0:row0 + 128, :],
                           waits=((ss, pnorm),))
                rdy = ln_stats(p, xt[:, :], D, stt, dump,
                               self.cm[:, 0:1], self.cm[:, 2:3], waits=(lv,))
                p.op("scalar", lambda: nc.scalar.activation(
                    out=lnb[:, :], in_=xt[:, :], func=AF.Identity,
                    scale=stt[:, 2:3], bias=stt[:, 3:4]),
                    waits=((sv, rdy), (sm, ptr)), inc=ss)
                pnorm = ss.n
                for g0 in (0, 8):
                    for j in range(8):
                        def ft(g0=g0, j=j):
                            return nc.tensor.transpose(
                                ps_t[:, j * 128:(j + 1) * 128],
                                lnb[:, (g0 + j) * 128:(g0 + j + 1) * 128],
                                self.eye[:])
                        p.op("tensor", ft, waits=((ss, pnorm),),
                             inc=(sm if j == 7 else None))
                    mmv = sm.n
                    ptr = mmv
                    for j in range(8):
                        def fe(g0=g0, j=j, tt=tt):
                            return nc.vector.tensor_copy(
                                lnT[:, (g0 + j) * L + tt * 128:
                                    (g0 + j) * L + (tt + 1) * 128],
                                ps_t[:, j * 128:(j + 1) * 128])
                        p.op("vector", fe, waits=((sm, mmv),),
                             inc=(sv if j == 7 else None))
                    p.op("tensor", lambda: None, waits=((sv, sv.n),))
            esLN.close()
            st_ip["prev_lnv"] = p.flush("vector")

        def ip_unit(b, fi, S):
            if fi >= 0:
                S["wv"] = p.dma("sync", wsl[:, :],
                                self.sp_inpT[fi].rearrange(
                                    "p a b -> p (a b)"),
                                waits=((sm, S["egv"]),))
            if fi >= 0:
                slot = fi % 2
                if fi < 32:
                    func = AF.Silu
                    dst = self.sp_sz[b][fi * 128:(fi + 1) * 128, :]
                elif fi < 40:
                    func = AF.Copy
                    dst = self.sp_x[b][(fi - 32) * 128:(fi - 31) * 128, :]
                elif fi < 48:
                    func = AF.Copy
                    dst = self.sp_B[b][(fi - 40) * 128:(fi - 39) * 128, :]
                else:
                    func = AF.Copy
                    dst = self.sp_C[b][(fi - 48) * 128:(fi - 47) * 128, :]
            for q in range(4):
                for dk in range(16):
                    def fm(dk=dk, q=q, fi=fi, b=b):
                        statT = (self.dtinT if fi < 0 else wsl)
                        off = q * 512
                        return nc.tensor.matmul(
                            psI[:, :],
                            statT[:, dk * 128:(dk + 1) * 128],
                            lnT[:, dk * L + off:dk * L + off + 512],
                            start=(dk == 0), stop=(dk == 15))
                    w = ()
                    if dk == 0:
                        w = (S["prev_lnv"], (ss, S["evv"][0]))
                        if q == 0 and fi >= 0:
                            w = w + (S["wv"],)
                    p.op("tensor", fm, waits=w,
                         inc=(sm if dk == 15 else None))
                S["egv"] = sm.n
                mmv = sm.n
                if fi < 0:
                    def fe(q=q, b=b):
                        return nc.scalar.activation(
                            out=dtlT[:, b * L + q * 512:
                                     b * L + (q + 1) * 512],
                            in_=psI[:, :], func=AF.Copy)
                    p.op("scalar", fe, waits=((sm, mmv),), inc=ss)
                    S["evv"][0] = ss.n
                    yield
                    continue

                def fe(slot=slot, func=func, q=q):
                    return nc.scalar.activation(
                        out=stg[slot][:, q * 512:(q + 1) * 512],
                        in_=psI[:, :], func=func)
                p.op("scalar", fe,
                     waits=((sm, mmv),) + tuple(S["sgv"][slot]), inc=ss)
                S["evv"][0] = ss.n
                if q == 3:
                    scv = ss.n
                    S["sgv"][slot] = (p.dma("gpsimd", dst, stg[slot][:, :],
                                            waits=((ss, scv),)),)
                yield

        emit_ln(0)
        for fi in [-1] + list(range(80)):
            for _ in ip_unit(0, fi, st_ip):
                pass
        emit_ln(1)
        p.fence()

        def ip_stream(b):
            for fi in [-1] + list(range(80)):
                yield from ip_unit(b, fi, st_ip)
        fill1 = ip_stream(1)
        for i in range(32):
            scan_dtile(0, i, st_scan, fill1)
        for _ in fill1:
            pass
        p.fence()
        es1.close()

        es4 = contextlib.ExitStack()
        sb4 = lambda n, s, d: es4.enter_context(nc.sbuf_tensor(n, s, d)).ap()
        psd = es4.enter_context(
            nc.psum_tensor("b3_ps", [128, 1024], F32)).ap()
        wph = sb4("b_wo", [128, 32 * 1024], BF16)
        ysl = sb4("b_ys0", [128, 32 * 128], BF16)
        xres = [sb4("b_xr0", [128, 1024], F32), sb4("b_xr1", [128, 1024], F32)]
        st_op = dict(yv=0, av=[0, 0], stv=[p.dma_marks("gpsimd"),
                                           p.dma_marks("gpsimd")],
                     wv=None, wu=0)

        def op_load_w(mh, S):
            S["wv"] = p.dma("sync", wph[:, :],
                            self.sp_woT[mh].rearrange("p a b -> p (a b)"),
                            waits=((sm, S["wu"]),))

        def op_unit(b, mh, tt, S):
            slot = tt % 2
            r0 = b * L
            lvy = p.dma("sync", ysl[:, :],
                        y3sp[b][tt].rearrange("cp ck t -> cp (ck t)"),
                        waits=((sm, S["yv"]),))
            xv = p.dma("sync", xres[slot][:, :],
                       self.x2b[r0 + tt * 128:r0 + (tt + 1) * 128,
                                mh * 1024:(mh + 1) * 1024],
                       waits=((sv, S["av"][slot]),) + tuple(S["stv"][slot]))
            psc = psd
            for c0 in range(0, 32, 8):
                for ck in range(c0, c0 + 8):
                    for ncc in range(2):
                        def fm(ck=ck, ncc=ncc, psc=psc):
                            return nc.tensor.matmul(
                                psc[:, ncc * 512:(ncc + 1) * 512],
                                ysl[:, ck * 128:(ck + 1) * 128],
                                wph[:, ck * 1024 + ncc * 512:
                                    ck * 1024 + (ncc + 1) * 512],
                                start=(ck == 0), stop=(ck == 31))
                        p.op("tensor", fm,
                             waits=(S["wv"], lvy, (sv, S["av"][0]),
                                    (sv, S["av"][1]))
                             if ck == 0 and ncc == 0 else (),
                             inc=(sm if ck == 31 and ncc == 1 else None))
                yield
            S["yv"] = sm.n
            S["wu"] = sm.n
            mmv = sm.n

            def fadd(slot=slot, psc=psc):
                return nc.vector.tensor_add(
                    xres[slot][:, :], psc[:, :], xres[slot][:, :])
            p.op("vector", fadd, waits=((sm, mmv), xv), inc=sv)
            S["av"][slot] = sv.n
            S["stv"][slot] = (p.dma(
                "gpsimd",
                self.x3b[r0 + tt * 128:r0 + (tt + 1) * 128,
                         mh * 1024:(mh + 1) * 1024],
                xres[slot][:, :], waits=((sv, S["av"][slot]),)),)

        def op_stream(b, S):
            for mh in range(2):
                op_load_w(mh, S)
                for tt in range(16):
                    yield from op_unit(b, mh, tt, S)
        fill2 = op_stream(0, st_op)
        for i in range(32):
            scan_dtile(1, i, st_scan, fill2)
        for _ in fill2:
            pass
        p.fence()
        for _ in op_stream(1, st_op):
            pass
        es4.close()
        es3.close()
        eso.close()

    def stub_copy(self, src, dst, tag):
        p, nc = self.p, self.nc
        es = contextlib.ExitStack()
        tmp = es.enter_context(nc.sbuf_tensor(f"{tag}_tmp", [128, D],
                                              F32)).ap()
        sv, ld, lg = p.sem("sv"), p.sem("ld"), p.sem("lg")
        prev = 0
        prev_st = (sv, 0)
        for tt in range(NTT):
            lv = p.dma("sync", tmp[:, :], src[tt * 128:(tt + 1) * 128, :],
                       waits=((sv, prev), prev_st))
            cv = p.op("vector", lambda: nc.vector.tensor_copy(
                tmp[:, :], tmp[:, :]), waits=(lv,), inc=sv)
            prev = cv
            prev_st = p.dma("gpsimd", dst[tt * 128:(tt + 1) * 128, :],
                            tmp[:, :], waits=((sv, cv),))
        es.close()

    def finalize(self):
        p = self.p
        p.op("gpsimd", lambda: None, waits=p.dma_marks())
        block_cm = self.nc.Block()
        block = block_cm.__enter__()
        p.build(block)
        block_cm.__exit__(None, None, None)
        self.es.close()
        return self.nc


STAGES = ("pa", "s1", "pf", "f1", "pb", "s3", "f2")


def build_program(debug=(), stages=None):
    stages = STAGES if stages is None else stages
    mk = MK(debug=debug)
    if "pa" in stages:
        mk.prologue_a()
        mk.p.barrier()
    if "s1" in stages:
        mk.s1_attn_a()
        mk.p.barrier()
    else:
        mk.stub_copy(mk.hs, mk.x1b, "s1")
        mk.p.barrier()
    if "pf" in stages:
        mk.prologue_ffn()
        mk.p.barrier()
    if "f1" in stages:
        mk.ffn(mk.x1b, mk.x2b, "f1")
        mk.p.barrier()
    else:
        mk.stub_copy(mk.x1b, mk.x2b, "s2")
        mk.p.barrier()
    if "pb" in stages:
        mk.prologue_b()
        mk.p.barrier()
    if "s3" in stages:
        mk.attn_b()
        mk.p.barrier()
    else:
        mk.stub_copy(mk.x2b, mk.x3b, "s3")
        mk.p.barrier()
    if "f2" in stages:
        mk.ffn(mk.x3b, mk.y_out, "f2")
    else:
        mk.stub_copy(mk.x3b, mk.y_out, "s4")
    return mk.finalize()


_NC_CACHE = {}


def _get_nc(debug=()):
    key = tuple(sorted(debug))
    if key not in _NC_CACHE:
        _NC_CACHE[key] = build_program(debug=key)
    return _NC_CACHE[key]


def kernel(**inputs):
    x = np.ascontiguousarray(inputs["L0_attn_a_hidden_states"])
    consts = _consts()
    nc = _get_nc()
    in_maps = []
    for c in range(NCORES):
        m = dict(consts)
        m["hs"] = np.ascontiguousarray(
            x[c * B_PER_CORE:(c + 1) * B_PER_CORE].reshape(TPC, D))
        for k, v in inputs.items():
            if k == "L0_attn_a_hidden_states":
                continue
            m[k] = v
        in_maps.append(m)
    res = run_bass_kernel_spmd(nc, in_maps, core_ids=list(range(NCORES)))
    out = np.concatenate(
        [r["y"].reshape(B_PER_CORE, L, D) for r in res.results], axis=0)
    return out


# version 15 (final)
# speedup vs baseline: 1.0125x; 1.0125x over previous
import contextlib
import numpy as np
import ml_dtypes
import concourse.bass as bass
import concourse.mybir as mybir
from concourse.bass_utils import run_bass_kernel_spmd

F32 = mybir.dt.float32
BF16 = mybir.dt.bfloat16
AF = mybir.ActivationFunctionType
ALU = mybir.AluOpType
AX = mybir.AxisListType

NCORES = 8
B_PER_CORE = 2
L = 2048
D = 2048
TPC = B_PER_CORE * L
NTT = TPC // 128

A_DI, A_H, A_N, A_K, A_DTR = 512, 32, 16, 4, 128
B_DI, B_DXB, B_N, B_DTR, B_K = 4096, 1024, 16, 128, 4
FF = 8192

ENGINES = ("sync", "tensor", "vector", "scalar", "gpsimd")


def _consts():
    bf = ml_dtypes.bfloat16
    eye = np.eye(128, dtype=np.float32).astype(bf)
    mx = np.zeros((32, 128), dtype=np.float32)
    for p_ in range(128):
        mx[16 * (p_ // 64) + p_ % 16, p_] = 1.0
    mb_ = np.zeros((32, 16 * 128), dtype=np.float32)
    for n in range(16):
        for p_ in range(128):
            mb_[n + 16 * (p_ // 64), n * 128 + p_] = 1.0
    mc = np.zeros((128, 16 * 128), dtype=np.float32)
    for n in range(16):
        for p_ in range(128):
            mc[16 * (p_ // 16) + n, n * 128 + p_] = 1.0
    md = np.zeros((32, 4 * 128), dtype=np.float32)
    for j in range(4):
        for p_ in range(128):
            md[8 * j + p_ // 16, j * 128 + p_] = 1.0
    ones = np.ones((128, 1), dtype=np.float32).astype(bf)
    c128 = np.full((128, 1), 128.0, dtype=np.float32)
    cmisc = np.zeros((128, 8), dtype=np.float32)
    cmisc[:, 0] = 1e-5
    cmisc[:, 1] = 1e-6
    cmisc[:, 2] = -0.5
    cmisc[:, 3] = -1e-6
    cmisc[:, 4] = 5e-7
    cmisc[:, 5] = 1.0
    onesr = np.ones((1, 128), dtype=np.float32)
    return dict(c_eye=eye, c_mx=mx.astype(bf), c_mb=mb_.astype(bf),
                c_mc=mc.astype(bf), c_md=md, c_ones=ones,
                c_128=c128, c_misc=cmisc, c_onesr=onesr)


class Sem:
    __slots__ = ("h", "n")

    def __init__(self, h):
        self.h = h
        self.n = 0

    def inc(self, k=1):
        self.n += k
        return self.n


class Prog:
    def __init__(self, nc, es):
        self.nc = nc
        self.es = es
        self.ops = []
        self._sems = {}
        self._lastwait = {}
        self._dma_rot = {}
        self._dirty = {e: False for e in ENGINES}

    def sem(self, name):
        if name not in self._sems:
            h = self.es.enter_context(self.nc.semaphore(name))
            self._sems[name] = Sem(h)
        return self._sems[name]

    def op(self, eng, fn, waits=(), inc=None, inc_k=1, kind="compute"):
        real = []
        for (s, v) in waits:
            if v <= 0:
                continue
            key = (eng, id(s))
            if self._lastwait.get(key, -1) >= v:
                continue
            self._lastwait[key] = v
            real.append((s, v))
        if inc is not None:
            inc.inc(inc_k)
            self._dirty[eng] = False
        elif kind == "compute":
            self._dirty[eng] = True
        self.ops.append((eng, fn, real, inc, inc_k, kind))
        return inc.n if inc is not None else None

    NDMASEM = 10

    def _dma_sems(self, q):
        key = "ld" if q == "sync" else "lg"
        return [self.sem(f"{key}{i}") for i in range(self.NDMASEM)]

    def dma(self, q, out_ap, in_ap, waits=(), **kw):
        sems = self._dma_sems(q)
        idx = self._dma_rot.get(q, 0)
        self._dma_rot[q] = (idx + 1) % self.NDMASEM
        sem = sems[idx]
        waits = tuple(waits) + ((sem, sem.n),)

        def fn(out_ap=out_ap, in_ap=in_ap, q=q, kw=kw):
            return getattr(self.nc, q).dma_start(out=out_ap, in_=in_ap, **kw)
        val = self.op(q, fn, waits=waits, inc=sem, inc_k=16, kind="dma")
        return (sem, val)

    def dma_marks(self, q=None):
        qs = ("sync", "gpsimd") if q is None else (q,)
        out = []
        for qq in qs:
            for s in self._dma_sems(qq):
                out.append((s, s.n))
        return tuple(out)

    ENG_MILESTONE = dict(tensor="sm", vector="sv", scalar="ss",
                         gpsimd="sp")

    def flush(self, eng):
        semname = self.ENG_MILESTONE.get(eng)
        if semname is None:
            return None
        sem = self.sem(semname)
        if not self._dirty[eng]:
            return (sem, sem.n)
        if eng == "tensor":
            def fn():
                return self.nc.tensor.nop()
        else:
            def fn(eng=eng):
                return getattr(self.nc, eng).drain()
        self.op(eng, fn, inc=sem, kind="raw")
        self._dirty[eng] = False
        return (sem, sem.n)

    def vmark(self, eng):
        return self.flush(eng)

    def fence(self, engines=ENGINES):
        marks = [self.flush(e) for e in ("tensor", "vector", "scalar",
                                         "gpsimd")]
        marks = [m for m in marks if m is not None] + list(self.dma_marks())
        for e in engines:
            self.op(e, lambda: None, waits=marks)

    def barrier(self):
        bar = self.sem("bar")
        for e in ("tensor", "vector", "scalar", "gpsimd"):
            self.flush(e)
        dma_marks = self.dma_marks()
        for e in ENGINES:
            def fn(e=e):
                return getattr(self.nc, e).sem_inc(bar.h, 1)
            self.op(e, fn, waits=dma_marks, kind="raw")
        bar.n += 5
        tgt = bar.n
        for e in ENGINES:
            self.op(e, lambda: None, waits=((bar, tgt),))

    def build(self, block):
        by_eng = {e: [] for e in ENGINES}
        for rec in self.ops:
            by_eng[rec[0]].append(rec)

        drain_engines = ("vector", "scalar", "gpsimd")

        def mk(e):
            def body(eng):
                for (_, fn, waits, inc, inc_k, kind) in by_eng[e]:
                    for (s, v) in waits:
                        eng.wait_ge(s.h, v)
                    ins = fn()
                    if kind == "compute" and ins is not None and                             e in drain_engines:
                        ins = eng.drain()
                    if inc is not None:
                        assert ins is not None
                        ins.then_inc(inc.h, inc_k)
            return body
        block.sync(mk("sync"))
        block.tensor(mk("tensor"))
        block.vector(mk("vector"))
        block.scalar(mk("scalar"))
        block.gpsimd(mk("gpsimd"))


def ln_stats(p, xt_ap, dlen, st, dump, eps, neghalf, waits=()):
    nc = p.nc
    sv, ss = p.sem("sv"), p.sem("ss")
    p.op("vector", lambda: nc.vector.reduce_sum(st[:, 0:1], xt_ap, axis=AX.X),
         waits=waits, inc=sv)
    p.op("scalar", lambda: nc.scalar.activation(
        out=dump[:, :dlen], in_=xt_ap, func=AF.Square, accum_out=st[:, 1:2]),
        waits=waits, inc=ss)
    vsq = ss.n
    inv = 1.0 / dlen
    p.op("vector", lambda: nc.vector.tensor_scalar(
        out=st[:, 0:1], in0=st[:, 0:1], scalar1=inv, scalar2=None,
        op0=ALU.mult))
    p.op("vector", lambda: nc.vector.tensor_scalar(
        out=st[:, 1:2], in0=st[:, 1:2], scalar1=inv, scalar2=None,
        op0=ALU.mult), waits=((ss, vsq),))
    p.op("vector", lambda: nc.vector.tensor_mul(
        st[:, 4:5], st[:, 0:1], st[:, 0:1]))
    p.op("vector", lambda: nc.vector.tensor_sub(
        st[:, 1:2], st[:, 1:2], st[:, 4:5]), inc=sv)
    vvar = sv.n
    p.op("scalar", lambda: nc.scalar.activation(
        out=st[:, 4:5], in_=st[:, 1:2], func=AF.Ln, bias=eps),
        waits=((sv, vvar),))
    p.op("scalar", lambda: nc.scalar.activation(
        out=st[:, 2:3], in_=st[:, 4:5], func=AF.Exp, scale=neghalf), inc=ss)
    vr = ss.n
    p.op("vector", lambda: nc.vector.tensor_scalar(
        out=st[:, 3:4], in0=st[:, 0:1], scalar1=st[:, 2:3], scalar2=-1.0,
        op0=ALU.mult, op1=ALU.mult), waits=((ss, vr),), inc=sv)
    return sv.n


def softplus(p, src_ap, bias_col, out_f32, tmp, waits=()):
    nc = p.nc
    sv, ss = p.sem("sv"), p.sem("ss")
    p.op("scalar", lambda: nc.scalar.activation(
        out=out_f32, in_=src_ap, func=AF.Identity, bias=bias_col),
        waits=waits)
    p.op("scalar", lambda: nc.scalar.activation(
        out=tmp, in_=out_f32, func=AF.Abs))
    p.op("scalar", lambda: nc.scalar.activation(
        out=tmp, in_=tmp, func=AF.Exp, scale=np.float32(-1.0)))
    p.op("scalar", lambda: nc.scalar.activation(
        out=tmp, in_=tmp, func=AF.Ln, bias=np.float32(1.0)), inc=ss)
    v1 = ss.n
    p.op("vector", lambda: nc.vector.tensor_scalar(
        out=out_f32, in0=out_f32, scalar1=0.0, scalar2=None, op0=ALU.max),
        waits=((ss, v1),))
    p.op("vector", lambda: nc.vector.tensor_add(out_f32, out_f32, tmp),
         inc=sv)
    return sv.n


class MK:

    def __init__(self, debug=()):
        self.debug = set(debug)
        nc = bass.Bass("TRN2", target_bir_lowering=False)
        self.nc = nc
        self.es = contextlib.ExitStack()
        self.p = Prog(nc, self.es)
        self._declare_io()
        self._declare_global_sbuf()

    def _declare_io(self):
        nc = self.nc
        inp = lambda n, s, d=F32: nc.dram_tensor(n, s, d,
                                                 kind="ExternalInput").ap()
        self.hs = inp("hs", [TPC, D])
        self.aWx = inp("L0_attn_a_W_in_x", [A_DI, D])
        self.aWz = inp("L0_attn_a_W_in_z", [A_DI, D])
        self.aCw = inp("L0_attn_a_conv_w", [A_DI, 1, A_K])
        self.aCb = inp("L0_attn_a_conv_b", [A_DI])
        self.aWp = inp("L0_attn_a_W_param", [A_DTR + 2 * A_H * A_N, A_DI])
        self.aWdt = inp("L0_attn_a_W_dt", [A_H, A_DTR])
        self.aBdt = inp("L0_attn_a_b_dt", [A_H])
        self.aAlog = inp("L0_attn_a_A_log", [A_H, A_N])
        self.aD = inp("L0_attn_a_D", [A_DI])
        self.aWo = inp("L0_attn_a_W_out", [D, A_DI])
        self.bWin = inp("L0_attn_b_in_proj_w", [2 * B_DXB + 2 * B_DI, D])
        self.bWdti = inp("L0_attn_b_dt_in_proj_w", [B_DTR, D])
        self.bWdtp = inp("L0_attn_b_dt_proj_w", [B_DI, B_DTR])
        self.bBdtp = inp("L0_attn_b_dt_proj_b", [B_DI])
        self.bCw = inp("L0_attn_b_conv_w", [B_DI, B_K])
        self.bCb = inp("L0_attn_b_conv_b", [B_DI])
        self.bAlog = inp("L0_attn_b_A_log", [B_DI, B_N])
        self.bD = inp("L0_attn_b_D", [B_DI])
        self.bWo = inp("L0_attn_b_out_proj_w", [D, B_DI])
        self.fWfc = inp("L0_ffn_w_fc", [FF, D])
        self.fWpj = inp("L0_ffn_w_proj", [D, FF])
        self.c_eye = inp("c_eye", [128, 128], BF16)
        self.c_mx = inp("c_mx", [32, 128], BF16)
        self.c_mb = inp("c_mb", [32, 16 * 128], BF16)
        self.c_mc = inp("c_mc", [128, 16 * 128], BF16)
        self.c_md = inp("c_md", [32, 4 * 128], F32)
        self.c_ones = inp("c_ones", [128, 1], BF16)
        self.c_128 = inp("c_128", [128, 1], F32)
        self.c_misc = inp("c_misc", [128, 8], F32)
        self.c_onesr = inp("c_onesr", [1, 128], F32)
        self.y_out = nc.dram_tensor("y", [TPC, D], F32,
                                    kind="ExternalOutput").ap()

        def scr(n, s, d):
            kind = "ExternalOutput" if n in self.debug else "Internal"
            return nc.dram_tensor(n, s, d, kind=kind).ap()
        self.x1b = scr("x1b", [TPC, D], F32)
        self.x2b = scr("x2b", [TPC, D], F32)
        self.x3b = scr("x3b", [TPC, D], F32)
        self.sp_inpT = scr("sp_inpT", [80, 128, 16, 128], BF16)
        self.sp_woT = scr("sp_woT", [2, 128, 32, 1024], BF16)
        self.sp_fcsT = scr("sp_fcsT", [4, 128, 16, D], BF16)
        self.sp_pjsT = scr("sp_pjsT", [2, 128, 64, 1024], BF16)
        self.sp_xhT = scr("sp_xhT", [NTT, 128, 16 * 128], BF16)
        self.sp_h = scr("sp_h", [TPC, FF], BF16)
        self.sp_hhT = scr("sp_hhT", [NTT, 128, 64 * 128], BF16)
        self.sp_sz = scr("sp_sz", [B_PER_CORE, B_DI, L], BF16)
        self.sp_x = scr("sp_x", [B_PER_CORE, B_DXB, L], BF16)
        self.sp_B = scr("sp_B", [B_PER_CORE, B_DXB, L], BF16)
        self.sp_C = scr("sp_C", [B_PER_CORE, B_DI, L], BF16)
        self.sp_y3 = scr("sp_y3", [B_PER_CORE * (L // 128) * B_DI * 128], BF16)
        self.dbg = scr("dbg", [128, 16384], F32)
        self.dbg2 = scr("dbg2", [128, 16384], BF16)

    def _declare_global_sbuf(self):
        nc, es = self.nc, self.es
        sb = lambda n, s, d: es.enter_context(nc.sbuf_tensor(n, s, d)).ap()
        self.eye = sb("eye", [128, 128], BF16)
        self.mx = sb("mx", [32, 128], BF16)
        self.mb = sb("mb", [32, 16 * 128], BF16)
        self.mc = sb("mc", [128, 16 * 128], BF16)
        self.md = sb("md", [32, 4 * 128], F32)
        self.ones = sb("ones", [128, 1], BF16)
        self.v128 = sb("v128", [128, 1], F32)
        self.cm = sb("cm", [128, 8], F32)
        self.onesr = sb("onesr", [1, 128], F32)
        self.qb = sb("qb", [128, 2], F32)
        self.gam = sb("gam", [128, 8], F32)
        self.dtinT = sb("dtinT", [128, 16 * 128], BF16)
        self.dtpjT = sb("dtpjT", [128, 32 * 128], BF16)
        self.b_Ab = sb("b_Ab", [128, 32 * 16], F32)
        self.b_misc = sb("b_misc", [128, 32 * 3], F32)
        self.b_convw = sb("b_convw", [128, 32 * 4], F32)

    def prep_transposed_sb(self, src, rows, cols, dst_sb, fstride, wload, wbf):
        p, nc = self.p, self.nc
        sv, ss, sm, ld = p.sem("sv"), p.sem("ss"), p.sem("sm"), p.sem("ld")
        for fi in range(rows // 128):
            lv = p.dma("sync", wload[:, :cols], src[fi * 128:(fi + 1) * 128, :],
                       waits=((ss, getattr(self, "_pt_cast", 0)),))
            p.op("scalar", lambda cols=cols: nc.scalar.activation(
                out=wbf[:, :cols], in_=wload[:, :cols], func=AF.Copy),
                waits=(lv, (sm, getattr(self, "_pt_tr", 0))), inc=ss)
            self._pt_cast = ss.n
            sc = ss.n
            nk = cols // 128
            dk = 0
            while dk < nk:
                g = min(8, nk - dk)
                for j in range(g):
                    def fn(dk=dk, j=j):
                        return nc.tensor.transpose(
                            self.ps_t[:, j * 128:(j + 1) * 128],
                            wbf[:, (dk + j) * 128:(dk + j + 1) * 128],
                            self.eye[:])
                    p.op("tensor", fn, waits=((ss, sc),),
                         inc=(p.sem("sm") if j == g - 1 else None))
                mmv = sm.n
                self._pt_tr = mmv
                for j in range(g):
                    def ev(dk=dk, j=j, fi=fi, fstride=fstride):
                        return nc.vector.tensor_copy(
                            dst_sb[:, (dk + j) * fstride + fi * 128:
                                   (dk + j) * fstride + (fi + 1) * 128],
                            self.ps_t[:, j * 128:(j + 1) * 128])
                    p.op("vector", ev, waits=((sm, mmv),),
                         inc=(sv if j == g - 1 else None))
                p.op("tensor", lambda: None, waits=((sv, sv.n),))
                dk += g

    def prologue_a(self):
        p, nc = self.p, self.nc
        sv, ss, sm, ld = p.sem("sv"), p.sem("ss"), p.sem("sm"), p.sem("ld")
        for dst, src in ((self.eye, self.c_eye), (self.mx, self.c_mx),
                         (self.mb, self.c_mb), (self.mc, self.c_mc),
                         (self.md, self.c_md), (self.ones, self.c_ones),
                         (self.v128, self.c_128), (self.cm, self.c_misc),
                         (self.onesr, self.c_onesr)):
            p.dma("sync", dst[:, :], src)
        p.op("scalar", lambda: nc.scalar.activation(
            out=self.qb[:, 0:1], in_=self.v128[:, :], func=AF.Identity,
            bias=self.cm[:, 3:4]), waits=p.dma_marks("sync"))
        p.op("scalar", lambda: nc.scalar.activation(
            out=self.qb[:, 1:2], in_=self.qb[:, 0:1], func=AF.Identity,
            scale=np.float32(-1.0)), inc=ss)

        es = self.s1_es = contextlib.ExitStack()
        sb = lambda n, s, d: es.enter_context(nc.sbuf_tensor(n, s, d)).ap()
        self.ps_t = es.enter_context(
            nc.psum_tensor("s1_ps_t", [128, 2048], BF16)).ap()
        self.ps_w = es.enter_context(
            nc.psum_tensor("s1_ps_w", [128, 2048], F32)).ap()
        self.ps_w2 = es.enter_context(
            nc.psum_tensor("s1_ps_w2", [128, 1024], F32)).ap()
        self.aWxT = sb("aWxT", [128, 16 * A_DI], BF16)
        self.aWzT = sb("aWzT", [128, 16 * A_DI], BF16)
        self.aWpT = sb("aWpT", [128, 4 * 1152], BF16)
        self.aWdtT = sb("aWdtT", [128, A_H], BF16)
        self.aWoT = sb("aWoT", [128, 4 * D], BF16)
        self.a_A = sb("a_A", [128, 4], F32)
        self.a_D = sb("a_D", [128, 4], F32)
        self.a_bdt = sb("a_bdt", [128, 1], F32)
        self.a_convw = sb("a_convw", [128, 16], F32)
        self.a_convb = sb("a_convb", [128, 4], F32)
        pa_tmp = contextlib.ExitStack()
        wload = pa_tmp.enter_context(nc.sbuf_tensor("pa_wload", [128, D],
                                                    F32)).ap()
        wbf = pa_tmp.enter_context(nc.sbuf_tensor("pa_wbf", [128, D],
                                                  BF16)).ap()

        self.prep_transposed_sb(self.aWx, A_DI, D, self.aWxT, A_DI, wload, wbf)
        self.prep_transposed_sb(self.aWz, A_DI, D, self.aWzT, A_DI, wload, wbf)
        self.prep_transposed_sb(self.aWp, 1152, A_DI, self.aWpT, 1152,
                                wload, wbf)
        self.prep_transposed_sb(self.aWo, D, A_DI, self.aWoT, D, wload, wbf)
        self.prep_transposed_sb(self.bWdti, B_DTR, D, self.dtinT, B_DTR,
                                wload, wbf)
        self.prep_transposed_sb(self.bWdtp, B_DI, B_DTR, self.dtpjT, B_DI,
                                wload, wbf)
        lv = p.dma("sync", wload[:32, :128], self.aWdt,
                   waits=((ss, self._pt_cast),))
        p.op("scalar", lambda: nc.scalar.activation(
            out=wbf[:32, :128], in_=wload[:32, :128], func=AF.Copy),
            waits=(lv, (sm, self._pt_tr)), inc=ss)
        scv = ss.n
        p.op("tensor", lambda: nc.tensor.transpose(
            self.ps_t[:, 0:32], wbf[:32, :128], self.eye[:32, :32]),
            waits=((ss, scv),), inc=sm)
        mmv = sm.n
        p.op("vector", lambda: nc.vector.tensor_copy(
            self.aWdtT[:, :32], self.ps_t[:, 0:32]), waits=((sm, mmv),),
            inc=sv)
        p.op("tensor", lambda: None, waits=((sv, sv.n),))

        slow = dict(allow_slow_non_contiguous=True)
        lva = p.dma("sync", self.a_A[:, :],
                    self.aAlog.rearrange("h n -> (h n)").rearrange(
                        "(j p) -> p j", p=128), **slow)
        p.op("scalar", lambda: nc.scalar.activation(
            out=self.a_A[:, :], in_=self.a_A[:, :], func=AF.Exp),
            waits=(lva,), inc=ss)
        p.op("vector", lambda: nc.vector.tensor_scalar(
            out=self.a_A[:, :], in0=self.a_A[:, :], scalar1=-1.0,
            scalar2=None, op0=ALU.mult), waits=((ss, ss.n),))
        p.dma("sync", self.a_D[:, :],
              self.aD.rearrange("(j p) -> p j", p=128), **slow)
        p.dma("sync", self.a_bdt[:32, 0:1],
              self.aBdt.rearrange("(h o) -> h o", o=1))
        p.dma("sync", self.a_convw[:, :].rearrange("p (j k) -> p j k", j=4),
              self.aCw.rearrange("c o k -> (c o) k").rearrange("(j p) k -> p j k", p=128), **slow)
        p.dma("sync", self.a_convb[:, :],
              self.aCb.rearrange("(j p) -> p j", p=128), **slow)
        lvab = p.dma("sync",
                     self.b_Ab[:, :].rearrange("p (i n) -> p i n", i=32),
                     self.bAlog.rearrange("(i p) n -> p i n", p=128), **slow)
        p.op("scalar", lambda: nc.scalar.activation(
            out=self.b_Ab[:, :], in_=self.b_Ab[:, :], func=AF.Exp),
            waits=(lvab,), inc=ss)
        p.op("vector", lambda: nc.vector.tensor_scalar(
            out=self.b_Ab[:, :], in0=self.b_Ab[:, :], scalar1=-1.0,
            scalar2=None, op0=ALU.mult), waits=((ss, ss.n),), inc=sv)
        p.dma("sync", self.b_misc[:, 0:32],
              self.bD.rearrange("(i p) -> p i", p=128), **slow)
        p.dma("sync", self.b_misc[:, 32:64],
              self.bCb.rearrange("(i p) -> p i", p=128), **slow)
        p.dma("sync", self.b_misc[:, 64:96],
              self.bBdtp.rearrange("(i p) -> p i", p=128), **slow)
        p.dma("sync", self.b_convw[:, :].rearrange("p (i k) -> p i k", i=32),
              self.bCw.rearrange("(i p) k -> p i k", p=128), **slow)
        pa_tmp.close()

    def s1_attn_a(self):
        p, nc = self.p, self.nc
        sv, ss, sm = p.sem("sv"), p.sem("ss"), p.sem("sm")
        ld, lg = p.sem("ld"), p.sem("lg")
        eye, ps_t, ps_w, ps_w2 = self.eye, self.ps_t, self.ps_w, self.ps_w2
        es = self.s1_es
        sb = lambda n, s, d: es.enter_context(nc.sbuf_tensor(n, s, d)).ap()
        TC = 512
        NCH = TPC // TC

        xt = sb("a_xt", [128, D], F32)
        dump = sb("a_dump", [128, D], BF16)
        st = sb("a_st", [128, 8], F32)
        lnb = sb("a_lnb", [128, D], BF16)
        lnT = sb("a_lnT", [128, 16 * TC], BF16)
        xaT = sb("a_xaT", [128, 4 * (TC + 3)], BF16)
        szT = sb("a_szT", [128, 4 * TC], BF16)
        xactT = sb("a_xactT", [128, 4 * TC], F32)
        xactTb = sb("a_xactTb", [128, 4 * TC], BF16)
        dtfT = sb("a_dtfT", [128, TC], BF16)
        BrT = sb("a_BrT", [128, 4 * TC], F32)
        CrT = sb("a_CrT", [128, 4 * TC], F32)
        deltaT = sb("a_deltaT", [128, TC], F32)
        sptmp = sb("a_sptmp", [128, TC], F32)
        abuf = sb("a_abuf", [128, 4 * TC], F32)
        hbuf = sb("a_hbuf", [128, 4 * TC], F32)
        bbuf = sb("a_bbuf", [128, 4 * TC], F32)
        carry = sb("a_carry", [128, 4], F32)
        y3T = sb("a_y3T", [128, 4 * TC], BF16)
        diag = sb("a_diag", [128, 16 * 128], BF16)
        xres = sb("a_xres", [128, D], F32)

        for ct in range(4):
            for k in range(4):
                def fn(ct=ct, k=k):
                    return nc.vector.tensor_scalar(
                        out=diag[:, (ct * 4 + k) * 128:(ct * 4 + k + 1) * 128],
                        in0=eye[:, :],
                        scalar1=self.a_convw[:, ct * 4 + k:ct * 4 + k + 1],
                        scalar2=None, op0=ALU.mult)
                p.op("vector", fn,
                     inc=(sv if (ct == 3 and k == 3) else None))
        diag_v = sv.n

        for ch in range(NCH):
            bi, ci = divmod(ch, 4)
            for tt in range(4):
                row0 = ch * TC + tt * 128
                lv = p.dma("sync", xt[:, :], self.hs[row0:row0 + 128, :],
                           waits=((ss, getattr(self, "_a_norm", 0)),))
                rdy = ln_stats(p, xt[:, :], D, st, dump,
                               self.cm[:, 0:1], self.cm[:, 2:3],
                               waits=(lv,))
                p.op("scalar", lambda: nc.scalar.activation(
                    out=lnb[:, :], in_=xt[:, :], func=AF.Identity,
                    scale=st[:, 2:3], bias=st[:, 3:4]),
                    waits=((sv, rdy), (sm, getattr(self, "_a_lntr", 0))),
                    inc=ss)
                self._a_norm = ss.n
                scv = ss.n
                for dk0 in (0, 8):
                    for j in range(8):
                        def fn(dk0=dk0, j=j):
                            return nc.tensor.transpose(
                                ps_t[:, j * 128:(j + 1) * 128],
                                lnb[:, (dk0 + j) * 128:(dk0 + j + 1) * 128],
                                eye[:])
                        p.op("tensor", fn, waits=((ss, scv),),
                             inc=(sm if j == 7 else None))
                    mmv = sm.n
                    self._a_lntr = mmv
                    for j in range(8):
                        def ev(dk0=dk0, j=j, tt=tt):
                            return nc.vector.tensor_copy(
                                lnT[:, (dk0 + j) * TC + tt * 128:
                                    (dk0 + j) * TC + (tt + 1) * 128],
                                ps_t[:, j * 128:(j + 1) * 128])
                        p.op("vector", ev, waits=((sm, mmv),),
                             inc=(sv if j == 7 else None))
                    p.op("tensor", lambda: None, waits=((sv, sv.n),))

            if ci == 0:
                for ct in range(4):
                    p.op("vector", lambda ct=ct: nc.vector.memset(
                        xaT[:, ct * (TC + 3):ct * (TC + 3) + 3], 0),
                        waits=((sm, getattr(self, "_a_convmm", 0)),),
                        inc=(sv if ct == 3 else None))
            else:
                for ct in range(4):
                    def cp(ct=ct):
                        return nc.vector.tensor_copy(
                            xaT[:, ct * (TC + 3):ct * (TC + 3) + 3],
                            xaT[:, ct * (TC + 3) + TC:ct * (TC + 3) + TC + 3])
                    p.op("vector", cp,
                         waits=((sm, getattr(self, "_a_convmm", 0)),),
                         inc=(sv if ct == 3 else None))
            pad_done = sv.n

            for ct in range(4):
                for dk in range(16):
                    def fn(ct=ct, dk=dk):
                        return nc.tensor.matmul(
                            ps_w[:, ct * 512:ct * 512 + 512],
                            self.aWxT[:, dk * A_DI + ct * 128:
                                      dk * A_DI + (ct + 1) * 128],
                            lnT[:, dk * TC:(dk + 1) * TC],
                            start=(dk == 0), stop=(dk == 15))
                    p.op("tensor", fn, inc=(sm if dk == 15 else None))
            mmx = sm.n
            for ct in range(4):
                def ev(ct=ct):
                    return nc.scalar.activation(
                        out=xaT[:, ct * (TC + 3) + 3:ct * (TC + 3) + 3 + TC],
                        in_=ps_w[:, ct * 512:(ct + 1) * 512], func=AF.Copy)
                p.op("scalar", ev, waits=((sm, mmx), (sv, pad_done)),
                     inc=(ss if ct == 3 else None))
            evx = ss.n
            p.op("tensor", lambda: None, waits=((ss, evx),))
            for ct in range(4):
                for dk in range(16):
                    def fn(ct=ct, dk=dk):
                        return nc.tensor.matmul(
                            ps_w[:, ct * 512:ct * 512 + 512],
                            self.aWzT[:, dk * A_DI + ct * 128:
                                      dk * A_DI + (ct + 1) * 128],
                            lnT[:, dk * TC:(dk + 1) * TC],
                            start=(dk == 0), stop=(dk == 15))
                    p.op("tensor", fn, inc=(sm if dk == 15 else None))
            mmz = sm.n
            for ct in range(4):
                def ev(ct=ct):
                    return nc.scalar.activation(
                        out=szT[:, ct * TC:(ct + 1) * TC],
                        in_=ps_w[:, ct * 512:(ct + 1) * 512], func=AF.Silu)
                p.op("scalar", ev,
                     waits=((sm, mmz), (sv, getattr(self, "_a_scn", 0))),
                     inc=(ss if ct == 3 else None))
            evz = ss.n
            p.op("tensor", lambda: None, waits=((ss, evz),))

            for ct in range(4):
                for k in range(4):
                    def fn(ct=ct, k=k):
                        return nc.tensor.matmul(
                            ps_w[:, ct * 512:(ct + 1) * 512],
                            diag[:, (ct * 4 + k) * 128:(ct * 4 + k + 1) * 128],
                            xaT[:, ct * (TC + 3) + k:ct * (TC + 3) + k + TC],
                            start=(k == 0), stop=(k == 3))
                    p.op("tensor", fn, waits=((sv, diag_v),),
                         inc=(sm if k == 3 else None))
            mmc = sm.n
            self._a_convmm = mmc
            for ct in range(4):
                def ev(ct=ct):
                    return nc.scalar.activation(
                        out=xactT[:, ct * TC:(ct + 1) * TC],
                        in_=ps_w[:, ct * 512:(ct + 1) * 512], func=AF.Silu,
                        bias=self.a_convb[:, ct:ct + 1])
                p.op("scalar", ev, waits=((sm, mmc),), inc=ss)
            evc = ss.n
            p.op("vector", lambda: nc.vector.tensor_copy(
                xactTb[:, :], xactT[:, :]),
                waits=((ss, evc), (sm, getattr(self, "_a_pmm", 0))), inc=sv)
            evcb = sv.n
            p.op("tensor", lambda: None, waits=((sv, evcb),))

            for fi in range(9):
                for ck in range(4):
                    def fn(fi=fi, ck=ck):
                        return nc.tensor.matmul(
                            ps_w[:, (fi % 4) * 512:(fi % 4) * 512 + 512],
                            self.aWpT[:, ck * 1152 + fi * 128:
                                      ck * 1152 + (fi + 1) * 128],
                            xactTb[:, ck * TC:(ck + 1) * TC],
                            start=(ck == 0), stop=(ck == 3))
                    p.op("tensor", fn, inc=(sm if ck == 3 else None))
                mmp = sm.n
                if fi == 8:
                    self._a_pmm = mmp
                if fi == 0:
                    def ev():
                        return nc.scalar.activation(
                            out=dtfT[:, :], in_=ps_w[:, 0:512], func=AF.Copy)
                elif fi <= 4:
                    def ev(fi=fi):
                        return nc.scalar.activation(
                            out=BrT[:, (fi - 1) * TC:fi * TC],
                            in_=ps_w[:, (fi % 4) * 512:(fi % 4) * 512 + 512],
                            func=AF.Copy)
                else:
                    def ev(fi=fi):
                        return nc.scalar.activation(
                            out=CrT[:, (fi - 5) * TC:(fi - 4) * TC],
                            in_=ps_w[:, (fi % 4) * 512:(fi % 4) * 512 + 512],
                            func=AF.Copy)
                p.op("scalar", ev, waits=((sm, mmp),), inc=ss)
                evp = ss.n
                p.op("tensor", lambda: None, waits=((ss, evp),))

            p.op("tensor", lambda: nc.tensor.matmul(
                ps_w2[:32, 0:512], self.aWdtT[:, :32], dtfT[:, :],
                start=True, stop=True), inc=sm)
            mmd = sm.n
            rdy = softplus(p, ps_w2[:32, 0:512], self.a_bdt[:32, 0:1],
                           deltaT[:32, :], sptmp[:32, :],
                           waits=((sm, mmd),))
            p.op("tensor", lambda: None, waits=((sv, rdy),))

            for ct in range(4):
                psr = ps_w2[:, 0:512] if ct % 2 == 0 else ps_w2[:, 512:1024]

                def fn(ct=ct, psr=psr):
                    return nc.tensor.matmul(
                        psr, self.md[:, ct * 128:(ct + 1) * 128],
                        deltaT[:32, :], start=True, stop=True)
                p.op("tensor", fn, inc=sm)
                mmr = sm.n

                def fa(ct=ct, psr=psr):
                    return nc.scalar.activation(
                        out=abuf[:, ct * TC:(ct + 1) * TC], in_=psr,
                        func=AF.Exp, scale=self.a_A[:, ct:ct + 1])
                p.op("scalar", fa, waits=((sm, mmr),), inc=ss)
                fav = ss.n

                def fb1(ct=ct):
                    return nc.vector.tensor_mul(
                        bbuf[:, ct * TC:(ct + 1) * TC],
                        BrT[:, ct * TC:(ct + 1) * TC],
                        xactT[:, ct * TC:(ct + 1) * TC])
                p.op("vector", fb1, waits=((sm, mmr),))

                def fb2(ct=ct, psr=psr):
                    return nc.vector.tensor_mul(
                        hbuf[:, ct * TC:(ct + 1) * TC],
                        bbuf[:, ct * TC:(ct + 1) * TC], psr)
                p.op("vector", fb2, waits=((ss, fav),), inc=sv)
                p.op("tensor", lambda: None, waits=((sv, sv.n), (ss, ss.n)))
            aexp = ss.n

            for ct in range(4):
                init = 0.0 if ci == 0 else carry[:, ct:ct + 1]

                def fs(ct=ct, init=init):
                    return nc.vector.tensor_tensor_scan(
                        out=bbuf[:, ct * TC:(ct + 1) * TC],
                        data0=abuf[:, ct * TC:(ct + 1) * TC],
                        data1=hbuf[:, ct * TC:(ct + 1) * TC],
                        initial=init, op0=ALU.mult, op1=ALU.add)
                p.op("vector", fs, waits=((ss, aexp),))

                def fc(ct=ct):
                    return nc.vector.tensor_copy(
                        carry[:, ct:ct + 1],
                        bbuf[:, ct * TC + TC - 1:ct * TC + TC])
                p.op("vector", fc)

                def fy(ct=ct):
                    return nc.vector.tensor_mul(
                        hbuf[:, ct * TC:(ct + 1) * TC],
                        bbuf[:, ct * TC:(ct + 1) * TC],
                        CrT[:, ct * TC:(ct + 1) * TC])
                p.op("vector", fy)

                def fy2(ct=ct):
                    return nc.vector.scalar_tensor_tensor(
                        out=bbuf[:, ct * TC:(ct + 1) * TC],
                        in0=xactT[:, ct * TC:(ct + 1) * TC],
                        scalar=self.a_D[:, ct:ct + 1],
                        in1=hbuf[:, ct * TC:(ct + 1) * TC],
                        op0=ALU.mult, op1=ALU.add)
                p.op("vector", fy2)

                def fy3(ct=ct):
                    return nc.vector.tensor_mul(
                        y3T[:, ct * TC:(ct + 1) * TC],
                        bbuf[:, ct * TC:(ct + 1) * TC],
                        szT[:, ct * TC:(ct + 1) * TC])
                p.op("vector", fy3, inc=(sv if ct == 3 else None))
            scn = sv.n
            self._a_scn = scn

            if ch == 0 and "dbg" in self.debug:
                for (off, buf, wid) in ((0, deltaT, TC), (512, abuf, 4 * TC),
                                        (2560, xactT, 4 * TC),
                                        (4608, szT, 4 * TC),
                                        (6656, hbuf, 4 * TC),
                                        (8704, CrT, 4 * TC)):
                    p.dma("gpsimd", self.dbg[:, off:off + wid],
                          buf[:, 0:wid], waits=((sv, scn),))
                p.dma("gpsimd", self.dbg2[:, 0:2048], y3T[:, 0:2048],
                      waits=((sv, scn),))
                p.dma("gpsimd", self.dbg2[:, 2048:4096],
                      self.aWoT[:, 0:2048], waits=((sv, scn),))

            for tt in range(4):
                row0 = ch * TC + tt * 128
                lv = p.dma("sync", xres[:, :], self.hs[row0:row0 + 128, :],
                           waits=((sv, getattr(self, "_a_add", 0)),
                                  getattr(self, "_a_st", None) or (sv, 0)))
                for ni in range(4):
                    for ck in range(4):
                        def fn(tt=tt, ni=ni, ck=ck):
                            return nc.tensor.matmul(
                                ps_w[:, ni * 512:(ni + 1) * 512],
                                y3T[:, ck * TC + tt * 128:
                                    ck * TC + (tt + 1) * 128],
                                self.aWoT[:, ck * D + ni * 512:
                                          ck * D + (ni + 1) * 512],
                                start=(ck == 0), stop=(ck == 3))
                        p.op("tensor", fn, waits=((sv, scn),),
                             inc=(sm if (ni == 3 and ck == 3) else None))
                mmo = sm.n

                def fadd():
                    return nc.vector.tensor_add(xres[:, :], ps_w[:, :],
                                                xres[:, :])
                p.op("vector", fadd, waits=((sm, mmo), lv), inc=sv)
                self._a_add = sv.n
                p.op("tensor", lambda: None, waits=((sv, sv.n),))
                self._a_st = p.dma("gpsimd", self.x1b[row0:row0 + 128, :],
                                   xres[:, :], waits=((sv, self._a_add),))

        self.s1_es.close()

    def prologue_ffn(self):
        p, nc = self.p, self.nc
        sv, ss, sm, ld, lg = (p.sem("sv"), p.sem("ss"), p.sem("sm"),
                              p.sem("ld"), p.sem("lg"))
        es = contextlib.ExitStack()
        sb = lambda n, s, d: es.enter_context(nc.sbuf_tensor(n, s, d)).ap()
        self.ps_t = es.enter_context(
            nc.psum_tensor("pf_ps_t", [128, 1024], BF16)).ap()
        self.ps_w2 = es.enter_context(
            nc.psum_tensor("pf_ps_w2", [128, 512], F32)).ap()
        wl = [sb("pf_wl0", [128, D], F32), sb("pf_wl1", [128, D], F32)]
        t1 = sb("pf_t1", [128, D], F32)
        t2 = sb("pf_t2", [128, D], F32)
        sgn = sb("pf_sgn", [128, D], BF16)
        tstage = sb("pf_ts", [128, D], BF16)
        acc = sb("pf_acc", [128, 2], F32)
        red = sb("pf_red", [128, 1], F32)
        p.op("vector", lambda: nc.vector.memset(acc[:, :], 0), inc=sv)

        def gamma_pass(w_src, nrow, col, subw):
            nsub = subw // D
            m0 = p.flush("vector")
            lvp = [m0, m0]
            cvp = 0
            for fi in range(nrow // 128):
                for s_ in range(nsub):
                    slot = (fi * nsub + s_) % 2
                    lv = p.dma("sync", wl[slot][:, :],
                               w_src[fi * 128:(fi + 1) * 128,
                                     s_ * D:(s_ + 1) * D],
                               waits=(lvp[slot],))

                    def fr(slot=slot):
                        return nc.vector.reduce_sum(
                            red[:, :], wl[slot][:, :], axis=AX.X,
                            apply_absolute_value=True)
                    p.op("vector", fr, waits=(lv,))

                    def fa(col=col):
                        return nc.vector.tensor_add(
                            acc[:, col:col + 1], acc[:, col:col + 1],
                            red[:, :])
                    cvp = p.op("vector", fa, inc=sv)
                    lvp[slot] = (sv, cvp)
        gamma_pass(self.fWfc, FF, 0, D)
        gamma_pass(self.fWpj, D, 1, FF)
        gv = sv.n
        p.op("tensor", lambda: nc.tensor.matmul(
            self.ps_w2[:1, 0:2], self.cm[:, 5:6], acc[:, :],
            start=True, stop=True), waits=((sv, gv),), inc=sm)
        mmv = sm.n
        p.op("scalar", lambda: nc.scalar.activation(
            out=self.gam[0:1, 6:8], in_=self.ps_w2[:1, 0:2], func=AF.Copy,
            scale=np.float32(1.0 / (FF * D))), waits=((sm, mmv),), inc=ss)
        scv = ss.n
        p.op("tensor", lambda: nc.tensor.matmul(
            self.ps_w2[:, 0:2], self.onesr[:, :], self.gam[0:1, 6:8],
            start=True, stop=True), waits=((ss, scv),), inc=sm)
        mmv = sm.n
        p.op("vector", lambda: nc.vector.tensor_copy(
            self.gam[:, 0:2], self.ps_w2[:, 0:2]), waits=((sm, mmv),), inc=sv)
        gv = sv.n
        for col in (0, 1):
            p.op("scalar", lambda col=col: nc.scalar.activation(
                out=self.gam[:, 2 + 2 * col:3 + 2 * col],
                in_=self.gam[:, col:col + 1], func=AF.Identity,
                scale=np.float32(0.5), bias=self.cm[:, 4:5]),
                waits=((sv, gv),))
            p.op("scalar", lambda col=col: nc.scalar.activation(
                out=self.gam[:, 3 + 2 * col:4 + 2 * col],
                in_=self.gam[:, 2 + 2 * col:3 + 2 * col], func=AF.Identity,
                scale=np.float32(-1.0)), inc=ss)
        thv = ss.n
        p.op("vector", lambda: None, waits=((ss, thv),))
        p.op("tensor", lambda: None, waits=((ss, thv),))

        def sign_pass(w_src, nrow, subw, thr_col, out_fn):
            nsub = subw // D
            m0 = p.flush("vector")
            lvp = [m0, m0]
            sgv = 0
            stv = p.dma_marks("gpsimd")
            trv = 0
            for fi in range(nrow // 128):
                for s_ in range(nsub):
                    slot = (fi * nsub + s_) % 2
                    lv = p.dma("sync", wl[slot][:, :],
                               w_src[fi * 128:(fi + 1) * 128,
                                     s_ * D:(s_ + 1) * D],
                               waits=(lvp[slot],))

                    def f1(slot=slot):
                        return nc.vector.tensor_scalar(
                            out=t1[:, :], in0=wl[slot][:, :],
                            scalar1=self.gam[:, thr_col:thr_col + 1],
                            scalar2=None, op0=ALU.is_gt)
                    p.op("vector", f1, waits=(lv,))

                    def f2(slot=slot):
                        return nc.vector.tensor_scalar(
                            out=t2[:, :], in0=wl[slot][:, :],
                            scalar1=self.gam[:, thr_col + 1:thr_col + 2],
                            scalar2=None, op0=ALU.is_lt)
                    p.op("vector", f2, inc=sv)
                    lvp[slot] = (sv, sv.n)
                    p.op("vector", lambda: nc.vector.tensor_sub(
                        sgn[:, :], t1[:, :], t2[:, :]),
                        waits=((sm, trv),), inc=sv)
                    sgv = sv.n
                    for g0 in (0, 8):
                        for j in range(8):
                            def ft(g0=g0, j=j):
                                return nc.tensor.transpose(
                                    self.ps_t[:, j * 128:(j + 1) * 128],
                                    sgn[:, (g0 + j) * 128:(g0 + j + 1) * 128],
                                    self.eye[:])
                            p.op("tensor", ft, waits=((sv, sgv),),
                                 inc=(sm if j == 7 else None))
                        mmv = sm.n
                        trv = mmv

                        def fe(g0=g0):
                            return nc.vector.tensor_copy(
                                tstage[:, g0 * 128:(g0 + 8) * 128],
                                self.ps_t[:, 0:1024])
                        p.op("vector", fe,
                             waits=((sm, mmv),) + tuple(stv), inc=sv)
                        p.op("tensor", lambda: None, waits=((sv, sv.n),))
                    stv = (p.dma("gpsimd", out_fn(fi, s_),
                                 tstage[:, :].rearrange(
                                     "p (dk t) -> p dk t", dk=16),
                                 waits=((sv, sv.n),)),)

        def fc_out(fi, s_):
            q, r = divmod(fi, 16)
            return self.sp_fcsT[q, :, :, r * 128:(r + 1) * 128]
        sign_pass(self.fWfc, FF, D, 2, fc_out)

        def pj_out(mi, fs):
            h, r = divmod(mi, 8)
            return self.sp_pjsT[h, :, fs * 16:(fs + 1) * 16,
                                r * 128:(r + 1) * 128]
        sign_pass(self.fWpj, D, FF, 4, pj_out)
        es.close()

    def ffn(self, x_in, x_out, tag):
        p, nc = self.p, self.nc
        sv, ss, sm, ld, lg = (p.sem("sv"), p.sem("ss"), p.sem("sm"),
                              p.sem("ld"), p.sem("lg"))

        es = contextlib.ExitStack()
        sb = lambda n, s, d: es.enter_context(nc.sbuf_tensor(n, s, d)).ap()
        ps_t = es.enter_context(
            nc.psum_tensor(f"{tag}a_pst", [128, 1024], BF16)).ap()
        xt = [sb(f"{tag}a_xt0", [128, D], F32), sb(f"{tag}a_xt1", [128, D], F32)]
        ln1 = [sb(f"{tag}a_ln10", [128, D], F32),
               sb(f"{tag}a_ln11", [128, D], F32)]
        xn2 = sb(f"{tag}a_xn2", [128, D], F32)
        dump = sb(f"{tag}a_dump", [128, D], BF16)
        st = [sb(f"{tag}a_st0", [128, 16], F32),
              sb(f"{tag}a_st1", [128, 16], F32)]
        xhb = [sb(f"{tag}a_xhb0", [128, D], BF16),
               sb(f"{tag}a_xhb1", [128, D], BF16)]
        tstage = sb(f"{tag}a_ts", [128, D], BF16)
        pv = dict(x=[0, 0], n2=[0, 0], q=[0, 0], tr=0, st=((sv, 0),))
        for tt in range(NTT):
            sl = tt % 2
            lv = p.dma("sync", xt[sl][:, :], x_in[tt * 128:(tt + 1) * 128, :],
                       waits=((ss, pv["x"][sl]),))
            rdy = ln_stats(p, xt[sl][:, :], D, st[sl][:, 0:8], dump,
                           self.cm[:, 0:1], self.cm[:, 2:3],
                           waits=(lv, (sv, pv["n2"][sl])))
            p.op("scalar", lambda sl=sl: nc.scalar.activation(
                out=ln1[sl][:, :], in_=xt[sl][:, :], func=AF.Identity,
                scale=st[sl][:, 2:3], bias=st[sl][:, 3:4]),
                waits=((sv, rdy),), inc=ss)
            pv["x"][sl] = ss.n
            rdy2 = ln_stats(p, ln1[sl][:, :], D, st[sl][:, 8:16], dump,
                            self.cm[:, 1:2], self.cm[:, 2:3],
                            waits=((ss, pv["x"][sl]),))
            p.op("vector", lambda sl=sl: nc.vector.tensor_scalar(
                out=xn2[:, :], in0=ln1[sl][:, :], scalar1=st[sl][:, 10:11],
                scalar2=st[sl][:, 11:12], op0=ALU.mult, op1=ALU.add),
                waits=((sv, rdy2),))
            p.op("vector", lambda sl=sl: nc.vector.reduce_max(
                st[sl][:, 12:13], xn2[:, :], axis=AX.X,
                apply_absolute_value=True))
            p.op("vector", lambda sl=sl: nc.vector.reciprocal(
                st[sl][:, 13:14], st[sl][:, 12:13]))
            p.op("vector", lambda sl=sl: nc.vector.tensor_scalar(
                out=st[sl][:, 13:14], in0=st[sl][:, 13:14], scalar1=128.0,
                scalar2=None, op0=ALU.mult))
            p.op("vector", lambda sl=sl: nc.vector.tensor_scalar(
                out=st[sl][:, 12:13], in0=st[sl][:, 12:13],
                scalar1=float(1.0 / 128.0), scalar2=None, op0=ALU.mult))
            p.op("vector", lambda sl=sl: nc.vector.tensor_scalar(
                out=xn2[:, :], in0=xn2[:, :], scalar1=st[sl][:, 13:14],
                scalar2=self.qb[:, 0:1], op0=ALU.mult, op1=ALU.min),
                inc=sv)
            pv["n2"][sl] = sv.n
            p.op("vector", lambda sl=sl: nc.vector.tensor_scalar(
                out=xhb[sl][:, :], in0=xn2[:, :], scalar1=self.qb[:, 1:2],
                scalar2=st[sl][:, 12:13], op0=ALU.max, op1=ALU.mult),
                waits=((sm, pv["q"][sl]),), inc=sv)
            qv = sv.n
            for g0 in (0, 8):
                for j in range(8):
                    def ft(g0=g0, j=j, sl=sl):
                        return nc.tensor.transpose(
                            ps_t[:, j * 128:(j + 1) * 128],
                            xhb[sl][:, (g0 + j) * 128:(g0 + j + 1) * 128],
                            self.eye[:])
                    p.op("tensor", ft, waits=((sv, qv),),
                         inc=(sm if j == 7 else None))
                mmv = sm.n
                pv["q"][sl] = mmv

                def fe(g0=g0):
                    return nc.vector.tensor_copy(
                        tstage[:, g0 * 128:(g0 + 8) * 128], ps_t[:, 0:1024])
                p.op("vector", fe, waits=((sm, mmv),) + tuple(pv["st"]),
                     inc=sv)
                p.op("tensor", lambda: None, waits=((sv, sv.n),))
            pv["st"] = (p.dma("gpsimd", self.sp_xhT[tt, :, :], tstage[:, :],
                              waits=((sv, sv.n),)),)
        es.close()
        p.fence()

        es = contextlib.ExitStack()
        sb = lambda n, s, d: es.enter_context(nc.sbuf_tensor(n, s, d)).ap()
        psp = [es.enter_context(
                   nc.psum_tensor(f"{tag}b_ps{i}", [128, 2048], F32)).ap()
               for i in range(2)]
        wq = sb(f"{tag}b_wq", [128, 16 * D], BF16)
        xsl = [sb(f"{tag}b_xs0", [128, 16 * 128], BF16),
               sb(f"{tag}b_xs1", [128, 16 * 128], BF16)]
        hst = [sb(f"{tag}b_h0", [128, D], BF16),
               sb(f"{tag}b_h1", [128, D], BF16)]
        xv = [0, 0]
        gv = [0, 0]
        hv = [p.dma_marks("gpsimd"), p.dma_marks("gpsimd")]
        for ncq in range(4):
            wv = p.dma("sync", wq[:, :],
                       self.sp_fcsT[ncq].rearrange("p a b -> p (a b)"),
                       waits=((sm, p.sem("sm").n),))
            for tt in range(NTT):
                slot = tt % 2
                lv = p.dma("sync", xsl[slot][:, :], self.sp_xhT[tt, :, :],
                           waits=((sm, xv[slot]),))
                psc = psp[slot]
                for dk in range(16):
                    for ncc in range(4):
                        def fm(dk=dk, ncc=ncc, slot=slot, psc=psc):
                            return nc.tensor.matmul(
                                psc[:, ncc * 512:(ncc + 1) * 512],
                                xsl[slot][:, dk * 128:(dk + 1) * 128],
                                wq[:, dk * D + ncc * 512:
                                   dk * D + (ncc + 1) * 512],
                                start=(dk == 0), stop=(dk == 15))
                        p.op("tensor", fm,
                             waits=(wv, lv, (ss, gv[slot])) if dk == 0
                             and ncc == 0 else (),
                             inc=(sm if dk == 15 and ncc == 3 else None))
                xv[slot] = sm.n
                mmv = sm.n

                def fg(slot=slot, psc=psc):
                    return nc.scalar.activation(
                        out=hst[slot][:, :], in_=psc[:, :], func=AF.Gelu,
                        scale=self.gam[:, 0:1])
                p.op("scalar", fg,
                     waits=((sm, mmv),) + tuple(hv[slot]), inc=ss)
                gv[slot] = ss.n
                ggv = ss.n
                hv[slot] = (p.dma("gpsimd",
                                  self.sp_h[tt * 128:(tt + 1) * 128,
                                            ncq * D:(ncq + 1) * D],
                                  hst[slot][:, :], waits=((ss, ggv),)),)
        es.close()
        p.fence()

        outer_es = contextlib.ExitStack()
        psq = [outer_es.enter_context(
                   nc.psum_tensor(f"{tag}d_ps{i}", [128, 512], F32)).ap()
               for i in range(4)]
        es = contextlib.ExitStack()
        sb = lambda n, s, d: es.enter_context(nc.sbuf_tensor(n, s, d)).ap()
        ps_t2 = [es.enter_context(
            nc.psum_tensor(f"{tag}c_pst{i}", [128, 1024], BF16)).ap()
            for i in range(2)]
        hrow = [sb(f"{tag}c_h0", [128, FF], BF16),
                sb(f"{tag}c_h1", [128, FF], BF16)]
        xn2h = sb(f"{tag}c_xn2", [128, FF], F32)
        st2 = [sb(f"{tag}c_st0", [128, 8], F32),
               sb(f"{tag}c_st1", [128, 8], F32)]
        hhb = sb(f"{tag}c_hhb", [128, FF], BF16)
        hh = [sb(f"{tag}cd_hh0", [128, FF], BF16),
              sb(f"{tag}cd_hh1", [128, FF], BF16)]
        wqd = sb(f"{tag}d_w", [128, 64 * 512], BF16)
        xres = [sb(f"{tag}d_x{i}", [128, 512], F32) for i in range(2)]

        wv = p.dma("sync", wqd[:, :].rearrange("p (a b) -> p a b", a=64),
                   self.sp_pjsT[0, :, :, 0:512],
                   waits=((sm, p.sem("sm").n),))
        st_marks = [None] * NTT
        stq = [((sv, 0),), ((sv, 0),)]
        mm2 = [0, 0]
        ev2 = [0, 0]
        favq = [0, 0, 0, 0]
        favx = [0, 0]
        xv2 = [None, None]
        xo2 = [((sv, 0),), ((sv, 0),)]
        mul_mark = [0, 0]
        rdy_s = [0, 0]
        lvp = [None, None]
        qv_prev = 0
        tr_last = 0
        femk = [0, 0]

        def stage_ls(t):
            sl = t % 2
            lvp[sl] = p.dma("sync", hrow[sl][:, :],
                            self.sp_h[t * 128:(t + 1) * 128, :],
                            waits=((ss, mul_mark[sl]),))
            w = (lvp[sl], (ss, mul_mark[sl]), (sm, mm2[sl]))
            if t >= 2:
                w = w + (st_marks[t - 2],)
            rdy_s[sl] = ln_stats(p, hrow[sl][:, :], FF, st2[sl], hh[sl],
                                 self.cm[:, 1:2], self.cm[:, 2:3], waits=w)

        def emit_lnapply(t):
            sl_ = t % 2
            p.op("scalar", lambda sl_=sl_: nc.scalar.activation(
                out=xn2h[:, :], in_=hrow[sl_][:, :], func=AF.Identity,
                scale=st2[sl_][:, 2:3], bias=st2[sl_][:, 3:4]),
                waits=((sv, rdy_s[sl_]), (sv, qv_prev)), inc=ss)
            mul_mark[sl_] = ss.n

        for tt in range(NTT):
            sl = tt % 2
            s4 = tt % 4
            if tt == 0:
                stage_ls(0)
            if tt + 1 < NTT:
                stage_ls(tt + 1)
            if tt == 0:
                emit_lnapply(0)
            xv2[sl] = p.dma("sync", xres[sl][:, :],
                            x_in[tt * 128:(tt + 1) * 128, 0:512],
                            waits=((sv, favx[sl]),) + xo2[sl])
            p.op("vector", lambda sl=sl: nc.vector.reduce_max(
                st2[sl][:, 5:6], xn2h[:, :], axis=AX.X,
                apply_absolute_value=True), waits=((ss, mul_mark[sl]),))
            p.op("vector", lambda sl=sl: nc.vector.reciprocal(
                st2[sl][:, 6:7], st2[sl][:, 5:6]))
            p.op("vector", lambda sl=sl: nc.vector.tensor_scalar(
                out=st2[sl][:, 6:7], in0=st2[sl][:, 6:7], scalar1=128.0,
                scalar2=None, op0=ALU.mult))
            p.op("vector", lambda sl=sl: nc.vector.tensor_scalar(
                out=st2[sl][:, 5:6], in0=st2[sl][:, 5:6],
                scalar1=float(1.0 / 128.0), scalar2=None, op0=ALU.mult))
            p.op("vector", lambda sl=sl: nc.vector.tensor_scalar(
                out=xn2h[:, :], in0=xn2h[:, :], scalar1=st2[sl][:, 6:7],
                scalar2=self.qb[:, 0:1], op0=ALU.mult, op1=ALU.min))
            p.op("vector", lambda sl=sl: nc.vector.tensor_scalar(
                out=hhb[:, :], in0=xn2h[:, :], scalar1=self.qb[:, 1:2],
                scalar2=st2[sl][:, 5:6], op0=ALU.max, op1=ALU.mult),
                waits=((sm, tr_last),), inc=sv)
            qv_prev = sv.n
            qvv = sv.n
            if tt + 1 < NTT:
                emit_lnapply(tt + 1)
            if tt > 0:
                pq = (tt - 1) % 4
                px = (tt - 1) % 2

                def fadd(pq=pq, px=px):
                    return nc.vector.scalar_tensor_tensor(
                        out=xres[px][:, :], in0=psq[pq][:, :],
                        scalar=self.gam[:, 1:2], in1=xres[px][:, :],
                        op0=ALU.mult, op1=ALU.add)
                p.op("vector", fadd, waits=((sm, mm2[px]), xv2[px]),
                     inc=sv)
                favq[pq] = favx[px] = sv.n
                xo2[px] = (p.dma("gpsimd",
                                 x_out[(tt - 1) * 128:tt * 128, 0:512],
                                 xres[px][:, :], waits=((sv, sv.n),)),)
            for g0 in range(0, 64, 8):
                bk = (g0 // 8) % 2
                for j in range(8):
                    def ft(g0=g0, j=j, bk=bk):
                        return nc.tensor.transpose(
                            ps_t2[bk][:, j * 128:(j + 1) * 128],
                            hhb[:, (g0 + j) * 128:(g0 + j + 1) * 128],
                            self.eye[:])
                    p.op("tensor", ft,
                         waits=((sv, qvv), (ss, femk[bk])),
                         inc=(sm if j == 7 else None))
                mmv = sm.n

                def fe(g0=g0, sl=sl, bk=bk):
                    return nc.scalar.activation(
                        out=hh[sl][:, g0 * 128:(g0 + 8) * 128],
                        in_=ps_t2[bk][:, 0:1024], func=AF.Copy)
                p.op("scalar", fe, waits=((sm, mmv),) + tuple(stq[sl]),
                     inc=ss)
                femk[bk] = ss.n
            tr_last = mmv
            ev2[sl] = ss.n
            st_marks[tt] = p.dma("gpsimd", self.sp_hhT[tt, :, :],
                                 hh[sl][:, :], waits=((ss, ev2[sl]),))
            stq[sl] = (st_marks[tt],)
            for fk in range(64):
                def fm(fk=fk, sl=sl, s4=s4):
                    return nc.tensor.matmul(
                        psq[s4][:, :], hh[sl][:, fk * 128:(fk + 1) * 128],
                        wqd[:, fk * 512:(fk + 1) * 512],
                        start=(fk == 0), stop=(fk == 63))
                p.op("tensor", fm,
                     waits=(wv, (ss, ev2[sl]),
                            (sv, favq[s4])) if fk == 0 else (),
                     inc=(sm if fk == 63 else None))
            mm2[sl] = sm.n
        pq, px = (NTT - 1) % 4, (NTT - 1) % 2
        p.op("vector", lambda pq=pq, px=px: nc.vector.scalar_tensor_tensor(
            out=xres[px][:, :], in0=psq[pq][:, :], scalar=self.gam[:, 1:2],
            in1=xres[px][:, :], op0=ALU.mult, op1=ALU.add),
            waits=((sm, mm2[px]), xv2[px]), inc=sv)
        favq[pq] = favx[px] = sv.n
        p.dma("gpsimd", x_out[(NTT - 1) * 128:NTT * 128, 0:512],
              xres[px][:, :], waits=((sv, sv.n),))
        es.close()
        p.fence()

        tes = contextlib.ExitStack()
        tsb = lambda n, s, d: tes.enter_context(nc.sbuf_tensor(n, s, d)).ap()
        wqt = tsb(f"{tag}t_w", [128, 64 * 1024], BF16)
        hh2 = [tsb(f"{tag}t_h0", [128, FF], BF16),
               tsb(f"{tag}t_h1", [128, FF], BF16)]
        xr2 = [tsb(f"{tag}t_x{i}", [128, 512], F32) for i in range(4)]
        xo4 = [((sv, 0),)] * 4
        xv4 = [None] * 4
        mmt2 = [0, 0]
        mmq4 = [0, 0, 0, 0]

        wv = p.dma("sync",
                   wqt[:, 0:64 * 512].rearrange("p (a b) -> p a b", a=64),
                   self.sp_pjsT[0, :, :, 512:1024],
                   waits=((sm, p.sem("sm").n),))
        for tt in range(NTT):
            sl = tt % 2
            s4 = tt % 4
            lw = ((sm, mmt2[sl]), st_marks[tt])
            lv1 = p.dma("sync", hh2[sl][:, 0:FF // 2],
                        self.sp_hhT[tt, :, 0:FF // 2], waits=lw)
            lv2 = p.dma("gpsimd", hh2[sl][:, FF // 2:],
                        self.sp_hhT[tt, :, FF // 2:], waits=lw)
            xv4[s4] = p.dma("sync", xr2[s4][:, :],
                            x_in[tt * 128:(tt + 1) * 128, 512:1024],
                            waits=((sv, favq[s4]),) + xo4[s4])
            for fk in range(64):
                def fm(fk=fk, sl=sl, s4=s4):
                    return nc.tensor.matmul(
                        psq[s4][:, :], hh2[sl][:, fk * 128:(fk + 1) * 128],
                        wqt[:, fk * 512:(fk + 1) * 512],
                        start=(fk == 0), stop=(fk == 63))
                if fk == 0:
                    w_ = (wv, lv1, (sv, favq[s4]))
                elif fk == 32:
                    w_ = (lv2,)
                else:
                    w_ = ()
                p.op("tensor", fm, waits=w_,
                     inc=(sm if fk == 63 else None))
            mmt2[sl] = mmq4[s4] = sm.n

            def fadd(s4=s4):
                return nc.vector.scalar_tensor_tensor(
                    out=xr2[s4][:, :], in0=psq[s4][:, :],
                    scalar=self.gam[:, 1:2], in1=xr2[s4][:, :],
                    op0=ALU.mult, op1=ALU.add)
            p.op("vector", fadd, waits=((sm, mmq4[s4]), xv4[s4]), inc=sv)
            favq[s4] = sv.n
            xo4[s4] = (p.dma("gpsimd",
                             x_out[tt * 128:(tt + 1) * 128, 512:1024],
                             xr2[s4][:, :], waits=((sv, favq[s4]),)),)

        wv = p.dma("sync",
                   wqt[:, :].rearrange("p (a b) -> p a b", a=64),
                   self.sp_pjsT[1, :, :, :],
                   waits=((sm, p.sem("sm").n),))
        for tt in range(NTT):
            sl = tt % 2
            lw = ((sm, mmt2[sl]), st_marks[tt])
            lv1 = p.dma("sync", hh2[sl][:, 0:FF // 2],
                        self.sp_hhT[tt, :, 0:FF // 2], waits=lw)
            lv2 = p.dma("gpsimd", hh2[sl][:, FF // 2:],
                        self.sp_hhT[tt, :, FF // 2:], waits=lw)
            for ncc in range(2):
                s4 = 2 * sl + ncc
                xv4[s4] = p.dma("sync", xr2[s4][:, :],
                                x_in[tt * 128:(tt + 1) * 128,
                                     (2 + ncc) * 512:(3 + ncc) * 512],
                                waits=((sv, favq[s4]),) + xo4[s4])
            for fk in range(64):
                for ncc in range(2):
                    s4 = 2 * sl + ncc

                    def fm(fk=fk, sl=sl, ncc=ncc, s4=s4):
                        return nc.tensor.matmul(
                            psq[s4][:, :],
                            hh2[sl][:, fk * 128:(fk + 1) * 128],
                            wqt[:, fk * 1024 + ncc * 512:
                                fk * 1024 + (ncc + 1) * 512],
                            start=(fk == 0), stop=(fk == 63))
                    if fk == 0 and ncc == 0:
                        w_ = (wv, lv1, (sv, favq[2 * sl]),
                              (sv, favq[2 * sl + 1]))
                    elif fk == 32 and ncc == 0:
                        w_ = (lv2,)
                    else:
                        w_ = ()
                    p.op("tensor", fm, waits=w_,
                         inc=(sm if fk == 63 and ncc == 1 else None))
            mmt2[sl] = sm.n
            for ncc in range(2):
                s4 = 2 * sl + ncc

                def fadd(s4=s4):
                    return nc.vector.scalar_tensor_tensor(
                        out=xr2[s4][:, :], in0=psq[s4][:, :],
                        scalar=self.gam[:, 1:2], in1=xr2[s4][:, :],
                        op0=ALU.mult, op1=ALU.add)
                p.op("vector", fadd, waits=((sm, mmt2[sl]), xv4[s4]),
                     inc=sv)
                favq[s4] = sv.n
                xo4[s4] = (p.dma("gpsimd",
                                 x_out[tt * 128:(tt + 1) * 128,
                                       (2 + ncc) * 512:(3 + ncc) * 512],
                                 xr2[s4][:, :], waits=((sv, favq[s4]),)),)
        tes.close()
        outer_es.close()
        p.fence()

    def prologue_b(self):
        p, nc = self.p, self.nc
        sv, ss, sm = p.sem("sv"), p.sem("ss"), p.sem("sm")
        es = contextlib.ExitStack()
        sb = lambda n, s, d: es.enter_context(nc.sbuf_tensor(n, s, d)).ap()
        self.ps_t = es.enter_context(
            nc.psum_tensor("pb_ps_t", [128, 1024], BF16)).ap()
        wl = [sb("pb_wl0", [128, D], F32), sb("pb_wl1", [128, D], F32)]
        wbf = sb("pb_wbf", [128, D], BF16)
        tstage = sb("pb_ts", [128, 32 * 128], BF16)

        def prep(w_src, nrow, ncol, out_fn):
            m0 = p.flush("vector")
            lvp = [m0, m0]
            cvp = 0
            trv = 0
            stv = p.dma_marks("gpsimd")
            nsub = ncol // D
            for fi in range(nrow // 128):
                for cs in range(nsub):
                    slot = (fi * nsub + cs) % 2
                    lv = p.dma("sync", wl[slot][:, :],
                               w_src[fi * 128:(fi + 1) * 128,
                                     cs * D:(cs + 1) * D],
                               waits=(lvp[slot],))
                    p.op("scalar", lambda slot=slot: nc.scalar.activation(
                        out=wbf[:, :], in_=wl[slot][:, :], func=AF.Copy),
                        waits=(lv, (sm, trv)), inc=ss)
                    lvp[slot] = (ss, ss.n)
                    scv = ss.n
                    for g0 in (0, 8):
                        for j in range(8):
                            def ft(g0=g0, j=j):
                                return nc.tensor.transpose(
                                    self.ps_t[:, j * 128:(j + 1) * 128],
                                    wbf[:, (g0 + j) * 128:(g0 + j + 1) * 128],
                                    self.eye[:])
                            p.op("tensor", ft, waits=((ss, scv),),
                                 inc=(sm if j == 7 else None))
                        mmv = sm.n
                        trv = mmv

                        def fe(g0=g0, cs=cs):
                            return nc.vector.tensor_copy(
                                tstage[:, (cs * 16 + g0) * 128:
                                       (cs * 16 + g0 + 8) * 128],
                                self.ps_t[:, 0:1024])
                        p.op("vector", fe, waits=((sm, mmv),) + tuple(stv),
                             inc=sv)
                        p.op("tensor", lambda: None, waits=((sv, sv.n),))
                    if cs == nsub - 1:
                        stv = (out_fn(fi),)

        def inp_out(fi):
            return p.dma("gpsimd", self.sp_inpT[fi, :, :, :],
                         tstage[:, 0:2048].rearrange("p (a b) -> p a b", a=16),
                         waits=((sv, p.sem("sv").n),))
        prep(self.bWin, 2 * B_DXB + 2 * B_DI, D, inp_out)

        def wo_out(mi):
            half, r = divmod(mi, 8)
            return p.dma("gpsimd",
                         self.sp_woT[half][:, :, r * 128:(r + 1) * 128],
                         tstage[:, :].rearrange("p (a b) -> p a b", a=32),
                         waits=((sv, p.sem("sv").n),))
        prep(self.bWo, D, B_DI, wo_out)
        es.close()

    def attn_b(self):
        p, nc = self.p, self.nc
        sv, ss, sm = p.sem("sv"), p.sem("ss"), p.sem("sm")
        sp = p.sem("sp")
        y3sp = self.sp_y3.rearrange("(b tb cp ck t) -> b tb cp ck t",
                                    b=B_PER_CORE, tb=16, cp=128, ck=32)
        eso = contextlib.ExitStack()
        dtlT = eso.enter_context(
            nc.sbuf_tensor("b_dtlT", [128, B_PER_CORE * L], BF16)).ap()

        es3 = contextlib.ExitStack()
        sb3 = lambda n, s, d: es3.enter_context(nc.sbuf_tensor(n, s, d)).ap()
        psB = [es3.enter_context(
                   nc.psum_tensor(f"b2_psb{j}", [128, 1024], F32)).ap()
               for j in range(2)]
        psC = [es3.enter_context(
                   nc.psum_tensor(f"b2_psc{j}", [128, 512], F32)).ap()
               for j in range(2)]
        dtT = sb3("b_dtT", [128, L], F32)
        xactT = sb3("b_xactT", [128, L], BF16)
        dtx = sb3("b_dtx", [128, L], BF16)
        abuf = [sb3("b_abuf0", [128, L], F32), sb3("b_abuf1", [128, L], F32)]
        bbuf = sb3("b_bbuf", [128, L], BF16)
        hbuf = sb3("b_hbuf", [128, L], F32)
        mbuf = [sb3("b_mbuf0", [128, L], F32), sb3("b_mbuf1", [128, L], F32)]
        ybuf = sb3("b_ybuf", [128, L], F32)
        xfull = sb3("b_xfull", [128, L + 3], BF16)
        xrows = sb3("b_xrows", [32, L], BF16)
        brows = sb3("b_brows", [32, L], BF16)
        crows = sb3("b_crows", [128, L], BF16)
        szrow = sb3("b_szrow", [128, L], BF16)
        y3b = sb3("b_y3b", [128, L], BF16)
        diag = sb3("b_diag", [128, 4 * 128], BF16)
        sptmp = abuf[1]

        st_scan = dict(
            bmh=[0, 0], scanv=[0, 0], scn_last=0, mh_last=0, mhC=[0, 0],
            ypool=[0, 0], prev_replv=0, prev_bbv=0, prev_ccv=0, prev_y3v=0,
            prev_xav=0, prev_lvx=(sv, 0), prev_lvz=(sv, 0),
            prev_lvb=(sv, 0), prev_lvc=(sv, 0),
            pv_y3st=p.dma_marks("gpsimd"))

        def scan_dtile(b, i, S, fill=None):
            for tc in range(4):
                def fm(tc=tc, i=i, b=b):
                    return nc.tensor.matmul(
                        psB[tc // 2][:, (tc % 2) * 512:(tc % 2 + 1) * 512],
                        self.dtpjT[:, i * 128:(i + 1) * 128],
                        dtlT[:, b * L + tc * 512:b * L + (tc + 1) * 512],
                        start=True, stop=True)
                p.op("tensor", fm,
                     waits=((ss, S["prev_xav"]), (sv, S["bmh"][0]),
                            (sv, S["bmh"][1])) if tc == 0 else (),
                     inc=(sm if tc == 3 else None))
            mmv = sm.n
            p.op("scalar", lambda i=i: nc.scalar.activation(
                out=dtT[:, 0:1024], in_=psB[0][:, :], func=AF.Identity,
                bias=self.b_misc[:, 64 + i:65 + i]),
                waits=((sm, mmv), (sv, S["scanv"][1])))
            p.op("scalar", lambda i=i: nc.scalar.activation(
                out=dtT[:, 1024:2048], in_=psB[1][:, :], func=AF.Identity,
                bias=self.b_misc[:, 64 + i:65 + i]), inc=ss)
            p.op("scalar", lambda: nc.scalar.activation(
                out=sptmp[:, :], in_=dtT[:, :], func=AF.Abs))
            p.op("scalar", lambda: nc.scalar.activation(
                out=sptmp[:, :], in_=sptmp[:, :], func=AF.Exp,
                scale=np.float32(-1.0)))
            p.op("scalar", lambda: nc.scalar.activation(
                out=sptmp[:, :], in_=sptmp[:, :], func=AF.Ln,
                bias=np.float32(1.0)), inc=ss)
            v1 = ss.n
            p.op("vector", lambda: nc.vector.tensor_scalar(
                out=dtT[:, :], in0=dtT[:, :], scalar1=0.0, scalar2=None,
                op0=ALU.max), waits=((ss, v1),))
            p.op("vector", lambda: nc.vector.tensor_add(
                dtT[:, :], dtT[:, :], sptmp[:, :]), inc=sv)
            rdy = sv.n
            lvx = p.dma("sync", xrows[:, :],
                        self.sp_x[b][32 * i:32 * i + 32, :],
                        waits=((sm, S["prev_replv"]), S["prev_lvx"]))
            S["prev_lvx"] = lvx
            for k in range(4):
                def fd(k=k, i=i):
                    return nc.vector.tensor_scalar(
                        out=diag[:, k * 128:(k + 1) * 128],
                        in0=self.eye[:, :],
                        scalar1=self.b_convw[:, 4 * i + k:4 * i + k + 1],
                        scalar2=None, op0=ALU.mult)
                p.op("vector", fd, inc=(sv if k == 3 else None))
            dgv = sv.n
            p.op("tensor", lambda: None, waits=((sv, rdy),))
            for tc in range(4):
                def fm(tc=tc):
                    return nc.tensor.matmul(
                        psB[tc // 2][:, (tc % 2) * 512:(tc % 2 + 1) * 512],
                        self.mx[:, :], xrows[:, tc * 512:(tc + 1) * 512],
                        start=True, stop=True)
                p.op("tensor", fm, waits=(lvx,) if tc == 0 else (),
                     inc=(sm if tc == 3 else None))
            mmv = sm.n
            S["prev_replv"] = mmv
            p.op("vector", lambda mmv=mmv: nc.vector.memset(
                xfull[:, 0:3], 0), waits=((sm, mmv),))
            p.op("scalar", lambda: nc.scalar.activation(
                out=xfull[:, 3:3 + 1024], in_=psB[0][:, :], func=AF.Copy),
                waits=((sm, mmv),))
            p.op("scalar", lambda: nc.scalar.activation(
                out=xfull[:, 3 + 1024:3 + L], in_=psB[1][:, :],
                func=AF.Copy), inc=ss)
            xfv = ss.n
            vfv = p.flush("vector")
            p.op("tensor", lambda: None, waits=((ss, xfv), vfv))
            for k in range(4):
                for tc in range(4):
                    def fm(k=k, tc=tc):
                        return nc.tensor.matmul(
                            psB[tc // 2][:, (tc % 2) * 512:
                                         (tc % 2 + 1) * 512],
                            diag[:, k * 128:(k + 1) * 128],
                            xfull[:, k + tc * 512:k + tc * 512 + 512],
                            start=(k == 0), stop=(k == 3))
                    p.op("tensor", fm, waits=((sv, dgv),) if k == 0
                         and tc == 0 else (),
                         inc=(sm if k == 3 and tc == 3 else None))
            mmv = sm.n
            p.op("scalar", lambda i=i: nc.scalar.activation(
                out=xactT[:, 0:1024], in_=psB[0][:, :], func=AF.Silu,
                bias=self.b_misc[:, 32 + i:33 + i]), waits=((sm, mmv),))
            p.op("scalar", lambda i=i: nc.scalar.activation(
                out=xactT[:, 1024:2048], in_=psB[1][:, :], func=AF.Silu,
                bias=self.b_misc[:, 32 + i:33 + i]), inc=ss)
            xav = ss.n
            S["prev_xav"] = xav
            p.op("vector", lambda: nc.vector.tensor_mul(
                dtx[:, :], dtT[:, :], xactT[:, :]),
                waits=((ss, xav),), inc=sv)
            dxv = sv.n
            p.op("tensor", lambda: None, waits=((sv, dxv),))
            lvz = p.dma("sync", szrow[:, :],
                        self.sp_sz[b][128 * i:128 * (i + 1), :],
                        waits=((sv, S["prev_y3v"]), S["prev_lvz"]))
            S["prev_lvz"] = lvz
            lvb = p.dma("sync", brows[:, :],
                        self.sp_B[b][32 * i:32 * i + 32, :],
                        waits=((sm, S["prev_bbv"]), S["prev_lvb"]))
            S["prev_lvb"] = lvb
            lvc = p.dma("sync", crows[:, :],
                        self.sp_C[b][128 * i:128 * (i + 1), :],
                        waits=((sm, S["prev_ccv"]), S["prev_lvc"]))
            S["prev_lvc"] = lvc

            for n in range(16):
                asl = n % 2

                def fa(i=i, n=n, asl=asl):
                    return nc.scalar.activation(
                        out=abuf[asl][:, :], in_=dtT[:, :], func=AF.Exp,
                        scale=self.b_Ab[:, 16 * i + n:16 * i + n + 1])
                p.op("scalar", fa, waits=((sv, S["scanv"][asl]),), inc=ss)
                aev = ss.n
                mmbh = [0, 0]
                for bh in range(2):
                    for tc in range(2):
                        def fb_(n=n, bh=bh, tc=tc):
                            return nc.tensor.matmul(
                                psB[bh][:, tc * 512:(tc + 1) * 512],
                                self.mb[:, n * 128:(n + 1) * 128],
                                brows[:, bh * 1024 + tc * 512:
                                      bh * 1024 + (tc + 1) * 512],
                                start=True, stop=True)
                        p.op("tensor", fb_,
                             waits=((sv, S["bmh"][bh]), lvb, (ss, xav))
                             if tc == 0 else (),
                             inc=(sm if tc == 1 else None))
                    mmbh[bh] = sm.n
                S["prev_bbv"] = mmbh[1]
                for bh in range(2):
                    def fb(n=n, bh=bh):
                        return nc.vector.tensor_mul(
                            bbuf[:, bh * 1024:(bh + 1) * 1024],
                            dtx[:, bh * 1024:(bh + 1) * 1024],
                            psB[bh][:, :])
                    p.op("vector", fb,
                         waits=((sm, mmbh[bh]), (sv, S["scn_last"])), inc=sv)
                    S["bmh"][bh] = sv.n
                if fill is not None:
                    next(fill, None)

                def fs(asl=asl):
                    return nc.vector.tensor_tensor_scan(
                        out=hbuf[:, :], data0=abuf[asl][:, :],
                        data1=bbuf[:, :], initial=0.0,
                        op0=ALU.mult, op1=ALU.add)
                p.op("vector", fs,
                     waits=((ss, aev), (sv, S["mh_last"])), inc=sv)
                S["scanv"][asl] = sv.n
                S["scn_last"] = sv.n
                msl = n % 2
                tgt = ybuf if n == 0 else mbuf[msl]
                for q in range(4):
                    cs = q % 2

                    def fcm(n=n, q=q, cs=cs):
                        return nc.tensor.matmul(
                            psC[cs][:, :],
                            self.mc[:, n * 128:(n + 1) * 128],
                            crows[:, q * 512:(q + 1) * 512],
                            start=True, stop=True)
                    p.op("tensor", fcm,
                         waits=((sv, S["mhC"][cs]), lvc), inc=sm)
                    mmc = sm.n
                    S["prev_ccv"] = mmc

                    def fmm(q=q, cs=cs, tgt=tgt):
                        return nc.vector.tensor_mul(
                            tgt[:, q * 512:(q + 1) * 512],
                            hbuf[:, q * 512:(q + 1) * 512],
                            psC[cs][:, :])
                    p.op("vector", fmm,
                         waits=((sm, mmc), (sp, S["ypool"][msl])) if n > 0
                         else ((sm, mmc), (sp, p.sem("sp").n)),
                         inc=sv)
                    S["mhC"][cs] = sv.n
                S["mh_last"] = sv.n
                if n > 0:
                    def fpool(msl=msl):
                        return nc.gpsimd.tensor_add(
                            ybuf[:, :], ybuf[:, :], mbuf[msl][:, :])
                    p.op("gpsimd", fpool, waits=((sv, S["mhC"][1]),), inc=sp)
                    S["ypool"][msl] = sp.n
            p.op("vector", lambda i=i: nc.vector.scalar_tensor_tensor(
                out=mbuf[0][:, :], in0=xactT[:, :],
                scalar=self.b_misc[:, i:i + 1], in1=ybuf[:, :],
                op0=ALU.mult, op1=ALU.add), waits=((sp, sp.n),))
            p.op("vector", lambda: nc.vector.tensor_mul(
                y3b[:, :], mbuf[0][:, :], szrow[:, :]),
                waits=(lvz,) + tuple(S["pv_y3st"]), inc=sv)
            y3v = sv.n
            S["prev_y3v"] = y3v
            S["pv_y3st"] = (p.dma(
                "gpsimd", y3sp[b][:, :, i, :].rearrange("tb cp t -> cp tb t"),
                y3b[:, :].rearrange("p (tb t) -> p tb t", tb=16),
                waits=((sv, y3v),)),)

        es1 = contextlib.ExitStack()
        sb = lambda n, s, d: es1.enter_context(nc.sbuf_tensor(n, s, d)).ap()
        psI = es1.enter_context(
            nc.psum_tensor("b1_psI", [128, 512], F32)).ap()
        lnT = sb("b_lnT", [128, 16 * L], BF16)
        wsl = sb("b_w0", [128, 16 * 128], BF16)
        stg = [sb("b_stg0", [128, L], BF16), sb("b_stg1", [128, L], BF16)]
        st_ip = dict(wv=None, egv=0, sgv=[p.dma_marks("gpsimd"),
                                          p.dma_marks("gpsimd")],
                     prev_lnv=0, evv=[0, 0])

        def emit_ln(b):
            esLN = contextlib.ExitStack()
            ps_t = esLN.enter_context(
                nc.psum_tensor(f"b1_pst_{b}", [128, 1024], BF16)).ap()
            xt = mbuf[0]
            dump = y3b
            stt = esLN.enter_context(
                nc.sbuf_tensor(f"b1_st_{b}", [128, 8], F32)).ap()
            lnb = xfull[:, 0:D]
            pnorm = 0
            ptr = 0
            r0 = b * L
            for tt in range(16):
                row0 = r0 + tt * 128
                lv = p.dma("sync", xt[:, :], self.x2b[row0:row0 + 128, :],
                           waits=((ss, pnorm),))
                rdy = ln_stats(p, xt[:, :], D, stt, dump,
                               self.cm[:, 0:1], self.cm[:, 2:3], waits=(lv,))
                p.op("scalar", lambda: nc.scalar.activation(
                    out=lnb[:, :], in_=xt[:, :], func=AF.Identity,
                    scale=stt[:, 2:3], bias=stt[:, 3:4]),
                    waits=((sv, rdy), (sm, ptr)), inc=ss)
                pnorm = ss.n
                for g0 in (0, 8):
                    for j in range(8):
                        def ft(g0=g0, j=j):
                            return nc.tensor.transpose(
                                ps_t[:, j * 128:(j + 1) * 128],
                                lnb[:, (g0 + j) * 128:(g0 + j + 1) * 128],
                                self.eye[:])
                        p.op("tensor", ft, waits=((ss, pnorm),),
                             inc=(sm if j == 7 else None))
                    mmv = sm.n
                    ptr = mmv
                    for j in range(8):
                        def fe(g0=g0, j=j, tt=tt):
                            return nc.vector.tensor_copy(
                                lnT[:, (g0 + j) * L + tt * 128:
                                    (g0 + j) * L + (tt + 1) * 128],
                                ps_t[:, j * 128:(j + 1) * 128])
                        p.op("vector", fe, waits=((sm, mmv),),
                             inc=(sv if j == 7 else None))
                    p.op("tensor", lambda: None, waits=((sv, sv.n),))
            esLN.close()
            st_ip["prev_lnv"] = p.flush("vector")

        def ip_unit(b, fi, S):
            if fi >= 0:
                S["wv"] = p.dma("sync", wsl[:, :],
                                self.sp_inpT[fi].rearrange(
                                    "p a b -> p (a b)"),
                                waits=((sm, S["egv"]),))
            if fi >= 0:
                slot = fi % 2
                if fi < 32:
                    func = AF.Silu
                    dst = self.sp_sz[b][fi * 128:(fi + 1) * 128, :]
                elif fi < 40:
                    func = AF.Copy
                    dst = self.sp_x[b][(fi - 32) * 128:(fi - 31) * 128, :]
                elif fi < 48:
                    func = AF.Copy
                    dst = self.sp_B[b][(fi - 40) * 128:(fi - 39) * 128, :]
                else:
                    func = AF.Copy
                    dst = self.sp_C[b][(fi - 48) * 128:(fi - 47) * 128, :]
            for q in range(4):
                for dk in range(16):
                    def fm(dk=dk, q=q, fi=fi, b=b):
                        statT = (self.dtinT if fi < 0 else wsl)
                        off = q * 512
                        return nc.tensor.matmul(
                            psI[:, :],
                            statT[:, dk * 128:(dk + 1) * 128],
                            lnT[:, dk * L + off:dk * L + off + 512],
                            start=(dk == 0), stop=(dk == 15))
                    w = ()
                    if dk == 0:
                        w = (S["prev_lnv"], (ss, S["evv"][0]))
                        if q == 0 and fi >= 0:
                            w = w + (S["wv"],)
                    p.op("tensor", fm, waits=w,
                         inc=(sm if dk == 15 else None))
                S["egv"] = sm.n
                mmv = sm.n
                if fi < 0:
                    def fe(q=q, b=b):
                        return nc.scalar.activation(
                            out=dtlT[:, b * L + q * 512:
                                     b * L + (q + 1) * 512],
                            in_=psI[:, :], func=AF.Copy)
                    p.op("scalar", fe, waits=((sm, mmv),), inc=ss)
                    S["evv"][0] = ss.n
                    yield
                    continue

                def fe(slot=slot, func=func, q=q):
                    return nc.scalar.activation(
                        out=stg[slot][:, q * 512:(q + 1) * 512],
                        in_=psI[:, :], func=func)
                p.op("scalar", fe,
                     waits=((sm, mmv),) + tuple(S["sgv"][slot]), inc=ss)
                S["evv"][0] = ss.n
                if q == 3:
                    scv = ss.n
                    S["sgv"][slot] = (p.dma("gpsimd", dst, stg[slot][:, :],
                                            waits=((ss, scv),)),)
                yield

        emit_ln(0)
        for fi in [-1] + list(range(80)):
            for _ in ip_unit(0, fi, st_ip):
                pass
        emit_ln(1)
        p.fence()

        def ip_stream(b):
            for fi in [-1] + list(range(80)):
                yield from ip_unit(b, fi, st_ip)
        fill1 = ip_stream(1)
        for i in range(32):
            scan_dtile(0, i, st_scan, fill1)
        for _ in fill1:
            pass
        p.fence()
        es1.close()

        es4 = contextlib.ExitStack()
        sb4 = lambda n, s, d: es4.enter_context(nc.sbuf_tensor(n, s, d)).ap()
        psd = es4.enter_context(
            nc.psum_tensor("b3_ps", [128, 1024], F32)).ap()
        wph = sb4("b_wo", [128, 32 * 1024], BF16)
        ysl = sb4("b_ys0", [128, 32 * 128], BF16)
        xres = [sb4("b_xr0", [128, 1024], F32), sb4("b_xr1", [128, 1024], F32)]
        st_op = dict(yv=0, av=[0, 0], stv=[p.dma_marks("gpsimd"),
                                           p.dma_marks("gpsimd")],
                     wv=None, wu=0)

        def op_load_w(mh, S):
            S["wv"] = p.dma("sync", wph[:, :],
                            self.sp_woT[mh].rearrange("p a b -> p (a b)"),
                            waits=((sm, S["wu"]),))

        def op_unit(b, mh, tt, S):
            slot = tt % 2
            r0 = b * L
            lvy = p.dma("sync", ysl[:, :],
                        y3sp[b][tt].rearrange("cp ck t -> cp (ck t)"),
                        waits=((sm, S["yv"]),))
            xv = p.dma("sync", xres[slot][:, :],
                       self.x2b[r0 + tt * 128:r0 + (tt + 1) * 128,
                                mh * 1024:(mh + 1) * 1024],
                       waits=((sv, S["av"][slot]),) + tuple(S["stv"][slot]))
            psc = psd
            for c0 in range(0, 32, 8):
                for ck in range(c0, c0 + 8):
                    for ncc in range(2):
                        def fm(ck=ck, ncc=ncc, psc=psc):
                            return nc.tensor.matmul(
                                psc[:, ncc * 512:(ncc + 1) * 512],
                                ysl[:, ck * 128:(ck + 1) * 128],
                                wph[:, ck * 1024 + ncc * 512:
                                    ck * 1024 + (ncc + 1) * 512],
                                start=(ck == 0), stop=(ck == 31))
                        p.op("tensor", fm,
                             waits=(S["wv"], lvy, (sv, S["av"][0]),
                                    (sv, S["av"][1]))
                             if ck == 0 and ncc == 0 else (),
                             inc=(sm if ck == 31 and ncc == 1 else None))
                yield
            S["yv"] = sm.n
            S["wu"] = sm.n
            mmv = sm.n

            def fadd(slot=slot, psc=psc):
                return nc.vector.tensor_add(
                    xres[slot][:, :], psc[:, :], xres[slot][:, :])
            p.op("vector", fadd, waits=((sm, mmv), xv), inc=sv)
            S["av"][slot] = sv.n
            S["stv"][slot] = (p.dma(
                "gpsimd",
                self.x3b[r0 + tt * 128:r0 + (tt + 1) * 128,
                         mh * 1024:(mh + 1) * 1024],
                xres[slot][:, :], waits=((sv, S["av"][slot]),)),)

        def op_stream(b, S):
            for mh in range(2):
                op_load_w(mh, S)
                for tt in range(16):
                    yield from op_unit(b, mh, tt, S)
        fill2 = op_stream(0, st_op)
        for i in range(32):
            scan_dtile(1, i, st_scan, fill2)
        for _ in fill2:
            pass
        p.fence()
        for _ in op_stream(1, st_op):
            pass
        es4.close()
        es3.close()
        eso.close()

    def stub_copy(self, src, dst, tag):
        p, nc = self.p, self.nc
        es = contextlib.ExitStack()
        tmp = es.enter_context(nc.sbuf_tensor(f"{tag}_tmp", [128, D],
                                              F32)).ap()
        sv, ld, lg = p.sem("sv"), p.sem("ld"), p.sem("lg")
        prev = 0
        prev_st = (sv, 0)
        for tt in range(NTT):
            lv = p.dma("sync", tmp[:, :], src[tt * 128:(tt + 1) * 128, :],
                       waits=((sv, prev), prev_st))
            cv = p.op("vector", lambda: nc.vector.tensor_copy(
                tmp[:, :], tmp[:, :]), waits=(lv,), inc=sv)
            prev = cv
            prev_st = p.dma("gpsimd", dst[tt * 128:(tt + 1) * 128, :],
                            tmp[:, :], waits=((sv, cv),))
        es.close()

    def finalize(self):
        p = self.p
        p.op("gpsimd", lambda: None, waits=p.dma_marks())
        block_cm = self.nc.Block()
        block = block_cm.__enter__()
        p.build(block)
        block_cm.__exit__(None, None, None)
        self.es.close()
        return self.nc


STAGES = ("pa", "s1", "pf", "f1", "pb", "s3", "f2")


def build_program(debug=(), stages=None):
    stages = STAGES if stages is None else stages
    mk = MK(debug=debug)
    if "pa" in stages:
        mk.prologue_a()
        mk.p.barrier()
    if "s1" in stages:
        mk.s1_attn_a()
        mk.p.barrier()
    else:
        mk.stub_copy(mk.hs, mk.x1b, "s1")
        mk.p.barrier()
    if "pf" in stages:
        mk.prologue_ffn()
        mk.p.barrier()
    if "f1" in stages:
        mk.ffn(mk.x1b, mk.x2b, "f1")
        mk.p.barrier()
    else:
        mk.stub_copy(mk.x1b, mk.x2b, "s2")
        mk.p.barrier()
    if "pb" in stages:
        mk.prologue_b()
        mk.p.barrier()
    if "s3" in stages:
        mk.attn_b()
        mk.p.barrier()
    else:
        mk.stub_copy(mk.x2b, mk.x3b, "s3")
        mk.p.barrier()
    if "f2" in stages:
        mk.ffn(mk.x3b, mk.y_out, "f2")
    else:
        mk.stub_copy(mk.x3b, mk.y_out, "s4")
    return mk.finalize()


_NC_CACHE = {}


def _get_nc(debug=()):
    key = tuple(sorted(debug))
    if key not in _NC_CACHE:
        _NC_CACHE[key] = build_program(debug=key)
    return _NC_CACHE[key]


def kernel(**inputs):
    x = np.ascontiguousarray(inputs["L0_attn_a_hidden_states"])
    consts = _consts()
    nc = _get_nc()
    in_maps = []
    for c in range(NCORES):
        m = dict(consts)
        m["hs"] = np.ascontiguousarray(
            x[c * B_PER_CORE:(c + 1) * B_PER_CORE].reshape(TPC, D))
        for k, v in inputs.items():
            if k == "L0_attn_a_hidden_states":
                continue
            m[k] = v
        in_maps.append(m)
    res = run_bass_kernel_spmd(nc, in_maps, core_ids=list(range(NCORES)))
    out = np.concatenate(
        [r["y"].reshape(B_PER_CORE, L, D) for r in res.results], axis=0)
    return out
